# version 09
# baseline (speedup 1.0000x reference)
import math
from contextlib import ExitStack

import numpy as np

import concourse.bass as bass
import concourse.mybir as mybir
from concourse.bass_utils import run_bass_kernel_spmd

F32 = mybir.dt.float32
BF16 = mybir.dt.bfloat16
AF = mybir.ActivationFunctionType
ALU = mybir.AluOpType

B, S, HID, NH, DH, INTER = 2, 4096, 4096, 32, 128, 11008
HC = HID // 128
IC = INTER // 128
SLOT_V = (2, 4, 6, 8, 1, 3, 5, 7)
COV_A, COV_B = 4096, 3584
NCOV = COV_A + COV_B
NKT = (8, 7)
SCALE = 1.0 / math.sqrt(DH)
EPS = 1e-5
NEG = -1e30
NT = 68
NG = 15


def _alibi_slopes(n=NH):
    cp2 = 2 ** math.floor(math.log2(n))
    base = 2.0 ** (-(2.0 ** (-(math.log2(cp2) - 3))))
    slopes = base ** np.arange(1, cp2 + 1, dtype=np.float64)
    if cp2 != n:
        extra_base = 2.0 ** (-(2.0 ** (-(math.log2(2 * cp2) - 3))))
        rem = min(cp2, n - cp2)
        extra = extra_base ** np.arange(1, 1 + 2 * rem, 2, dtype=np.float64)
        slopes = np.concatenate([slopes, extra])
    return slopes.astype(np.float64)


SLOPES = _alibi_slopes()


def _att_cfg():
    cfg = []
    for X in (0, 1):
        vs = [SLOT_V[s] for s in range(4 * X, 4 * X + 4)]
        cfg.append([([j for j, v in enumerate(vs) if v >= kt + 1][0],
                     len([j for j, v in enumerate(vs) if v >= kt + 1]))
                    for kt in range(NKT[X])])
    return cfg


CFG = _att_cfg()

RP_OFF = {}
_o = 0
for _X in (0, 1):
    for _kt in range(NKT[_X]):
        RP_OFF[(_X, _kt)] = _o
        _o += 4 * CFG[_X][_kt][1] * 128
RP_TOT = _o

M = {}
_c = 16512
def _put(name, nbytes):
    global _c
    M[name] = _c
    _c = (_c + nbytes + 31) // 32 * 32

_put("ones_bf", 256); _put("I_bf", 256); _put("idf", 512); _put("onesf", 512)
_put("ag_t", 512); _put("au_t", 512); _put("agnat", 512)
_put("bst", 384); _put("mv", 64); _put("varp", 32); _put("lcol", 32); _put("rstd", 32)
_put("rzbf", 2048); _put("rzrow", 2048)
assert _c <= 24704, _c
M["xn"] = 24704
M["xtile"] = 90240
M["xntok"] = 123008
M["xnTcov"] = 139392
M["wkvb"] = 172160
M["vtstg"] = 188544
M["ktsb"] = 198624
M["vkm"] = 213984
M["rpsb"] = 90240
M["wqb"] = 163968
M["qtb"] = 172160
M["usb"] = 176256
M["ptb"] = 188544
M["atstg"] = 194688
M["rzbsb"] = M["ag_t"]
assert M["atstg"] + 2048 <= 198624
M["wgb"] = 90240
M["wub"] = 106656
M["sgs"] = 123072
M["hstg"] = 139488
assert 139488 + 16384 <= 229344
M["hblk"] = 24704
M["atblk"] = 112800
M["adb"] = 145600
M["adrow"] = 162016
M["wdb"] = 178432
M["wob"] = 194848
M["gbuf"] = 203072
M["g2buf"] = 207200
M["xeb"] = 211328
M["oeb"] = 219552
M["sgb"] = 223680
assert 223680 + 4096 <= 229344


class ETrack:

    def __init__(self, stack, nc, name, semc):
        self.h = stack.enter_context(nc.semaphore(name))
        self.n = 0
        self.last = None
        self.semc = semc

    def inc(self, ins, sm, name, k=1):
        ins.then_inc(sm[name], k)
        self.semc[name] = self.semc.get(name, 0) + k
        self.last = (sm[name], self.semc[name])
        return ins

    def mark(self, ins):
        self.n += 1
        ins.then_inc(self.h, 1)
        self.last = (self.h, self.n)
        return ins

    def wait_self(self, eng):
        if self.last is not None:
            eng.wait_ge(self.last[0], self.last[1])


def build_nc(dbg=False):
    nc = bass.Bass("TRN2", target_bir_lowering=False)

    x_own = nc.dram_tensor("x_own", [1024, HID], F32, kind="ExternalInput").ap()
    x_cov = nc.dram_tensor("x_cov", [NCOV, HID], F32, kind="ExternalInput").ap()
    wq4 = nc.dram_tensor("wq4", [NH, 128, HC, 128], F32, kind="ExternalInput").ap()
    wkv4 = nc.dram_tensor("wkv4", [2, 128, HC, 128], F32, kind="ExternalInput").ap()
    wg4 = nc.dram_tensor("wg4", [IC, 128, HC, 128], F32, kind="ExternalInput").ap()
    wu4 = nc.dram_tensor("wu4", [IC, 128, HC, 128], F32, kind="ExternalInput").ap()
    wdt = nc.dram_tensor("wdt", [INTER, HID], F32, kind="ExternalInput").ap()
    wo = nc.dram_tensor("wo", [HID, HID], F32, kind="ExternalInput").ap()
    agd = nc.dram_tensor("agd", [IC, 128], F32, kind="ExternalInput").ap()
    aud = nc.dram_tensor("aud", [IC, 128], F32, kind="ExternalInput").ap()
    add_ = nc.dram_tensor("add_", [HID], F32, kind="ExternalInput").ap()
    rp = nc.dram_tensor("rp", [128, RP_TOT], F32, kind="ExternalInput").ap()
    ident = nc.dram_tensor("ident", [128, 128], F32, kind="ExternalInput").ap()
    out = nc.dram_tensor("out", [1024, HID], F32, kind="ExternalOutput").ap()
    ikind = "ExternalOutput" if dbg else "Internal"
    wdt_bf = nc.dram_tensor("wdt_bf", [INTER, HID], BF16, kind="Internal").ap()
    wo_bf = nc.dram_tensor("wo_bf", [HID, HID], BF16, kind="Internal").ap()
    h_d = nc.dram_tensor("h_d", [8, 128, IC, 128], BF16, kind=ikind).ap()
    at_d = nc.dram_tensor("at_d", [2, 128, NH, 512], BF16, kind=ikind).ap()
    if dbg:
        xn_d = nc.dram_tensor("xn_d", [128, HC * 1024], BF16, kind="ExternalOutput").ap()
        kt_d = nc.dram_tensor("kt_d", [128, NCOV], BF16, kind="ExternalOutput").ap()
        vk_d = nc.dram_tensor("vk_d", [128, NCOV], BF16, kind="ExternalOutput").ap()

    def sb(name, shape, dt):
        return nc.alloc_sbuf_tensor_at(name, shape, dt, offset=M[name]).ap()

    ones_bf = sb("ones_bf", [128, 128], BF16)
    I_bf = sb("I_bf", [128, 128], BF16)
    idf = sb("idf", [128, 128], F32)
    onesf = sb("onesf", [128, 128], F32)
    ag_t = sb("ag_t", [128, 128], F32)
    au_t = sb("au_t", [128, 128], F32)
    agnat = sb("agnat", [128, 128], F32)
    bst = sb("bst", [128, 2, 48], F32)
    mv = sb("mv", [128, 2, 2], F32)
    varp = sb("varp", [128, 2, 1], F32)
    lcol = sb("lcol", [128, 2, 1], F32)
    rstd = sb("rstd", [128, 2, 1], F32)
    rzrow = sb("rzrow", [1, 512], F32)
    rzbf = sb("rzbf", [1, 2, 512], BF16)
    xn = sb("xn", [128, HC * 1024], BF16)
    xtile = sb("xtile", [128, 2, HID], F32)
    xntok = sb("xntok", [128, 2, HID], BF16)
    xnTcov = sb("xnTcov", [128, HC * 512], BF16)
    wkvb = sb("wkvb", [128, 2, HC, 128], BF16)
    vtstg = sb("vtstg", [128, 2, 512], BF16)
    ktsb = sb("ktsb", [128, NCOV], BF16)
    vkm = sb("vkm", [128, NCOV], BF16)
    rpsb = sb("rpsb", [128, RP_TOT], F32)
    wqb = sb("wqb", [128, HC, 128], BF16)
    qtb = sb("qtb", [128, 2, 1024], BF16)
    usb = sb("usb", [128, 3, 2, 512], F32)
    ptb = sb("ptb", [128, 3, 2, 512], BF16)
    atstg = sb("atstg", [128, 2, 512], BF16)
    rzbsb = sb("rzbsb", [128, 2, 512], BF16)
    atn2 = atstg
    wgb = sb("wgb", [128, 2, HC, 128], BF16)
    wub = sb("wub", [128, 2, HC, 128], BF16)
    sgs = sb("sgs", [128, 2, 2, 512], F32)
    hstg = sb("hstg", [128, 2, 4, 1024], BF16)
    hblk = [nc.alloc_sbuf_tensor_at(f"hblk{i}", [128, IC, 128], BF16,
                                    offset=M["hblk"] + i * 22016).ap()
            for i in range(4)]
    atblk = sb("atblk", [128, NH, 512], BF16)
    adb = sb("adb", [128, HID], F32)
    adrow = sb("adrow", [1, HID], F32)
    wdb = sb("wdb", [128, 16, 512], BF16)
    wob = sb("wob", [128, 8, 512], BF16)
    gbuf = sb("gbuf", [128, 2, 512], F32)
    g2buf = sb("g2buf", [128, 2, 512], F32)
    xeb = sb("xeb", [128, 4, 512], F32)
    oeb = sb("oeb", [128, 2, 512], F32)
    sgb = sb("sgb", [128, 2, 512], F32)

    P = nc.alloc_psum_tensor("P", [128, 8, 512], F32).ap()
    xnv = xn.rearrange("p (c t) -> p c t", t=1024)
    xcv = xnTcov.rearrange("p (c t) -> p c t", t=512)

    est = ExitStack()
    SM = {}
    semnames = ["s_x0", "s_x1", "s_x0b", "s_x1b", "s_stat", "s_rstd", "s_xn", "s_trp", "s_trev",
                "s_kv", "s_kvev", "s_vt", "s_wkv", "s_ib",
                "b_p1", "s_rp", "s_wq", "s_stepdone", "s_stepfree", "s_qpmm", "s_qtr",
                "s_pt", "s_ptfree", "s_zev", "s_rz", "s_rzmm",
                "s_rzb", "s_atev", "s_atn2", "s_atsp0", "s_atsp1",
                "b_p2", "s_agt", "s_agmm", "s_agev", "s_wg0", "s_wg1", "s_wu0",
                "s_wu1", "s_mm3", "s_sil", "s_h3", "s_hsp0", "s_hsp1",
                "b_p3", "s_hld", "s_atld", "s_adr", "s_adm", "s_adb",
                "s_wdc", "s_woc", "s_dbgd", "s_cvt",
                "s_f4", "s_s4", "s_gb", "s_g2", "s_sgb", "s_xe0", "s_xe1", "s_xe2", "s_xe3",
                "s_oe", "s_out0", "s_out1"]
    semnames += [f"s_wd{j}" for j in range(8)] + [f"s_wo{j}" for j in range(8)]
    for s_ in semnames:
        SM[s_] = est.enter_context(nc.semaphore(s_))

    SEMC = {}
    tv = ETrack(est, nc, "ss_v", SEMC)
    ta = ETrack(est, nc, "ss_a", SEMC)
    tp = ETrack(est, nc, "ss_p", SEMC)

    steps = []
    for h in range(NH):
        for X in (0, 1):
            for kt in range(NKT[X]):
                for pr in (0, 1):
                    steps.append(("pr", h, X, kt, pr))
    NSTEP = len(steps)
    NSH = 30
    pair_idx = {}
    _pi = 0
    for stp in steps:
        if stp[0] == "pr":
            pair_idx[stp[1:]] = _pi
            _pi += 1
    gidx = {}
    _gi = 0
    for h in range(NH):
        for X in (0, 1):
            gidx[(h, X)] = _gi
            _gi += 1

    with nc.Block() as block:

        @block.sync
        def _(q):
            for i in range(NT):
                if i >= 2:
                    q.wait_ge(SM["s_xn"], i - 1)
                src = (x_own[i * 128:(i + 1) * 128, :] if i < 8
                       else x_cov[(i - 8) * 128:(i - 7) * 128, :])
                q.dma_start(xtile[:, i % 2, 0:2048], src[:, 0:2048]
                            ).then_inc(SM[f"s_x{i % 2}"], 16)
            q.wait_ge(SM["b_p1"], 4)
            q.dma_start(rpsb[:], rp[:, :]).then_inc(SM["s_rp"], 16)
            for h in range(NH):
                for X in (0, 1):
                    g = gidx[(h, X)]
                    q.wait_ge(SM["s_atn2"], g + 1)
                    q.dma_start(at_d[X, :, h, :], atn2[:, X, :]
                                ).then_inc(SM[f"s_atsp{X}"], 16)
            if dbg:
                q.wait_ge(SM["b_p2"], 4)
                q.dma_start(xn_d[:, :], xn[:]).then_inc(SM["s_dbgd"], 16)
                q.dma_start(kt_d[:, :], ktsb[:]).then_inc(SM["s_dbgd"], 16)
                q.dma_start(vk_d[:, :], vkm[:]).then_inc(SM["s_dbgd"], 16)
                q.wait_ge(SM["s_dbgd"], 48)
            nbatch = (IC + 3) // 4
            for b in range(nbatch):
                nch = min(4, IC - b * 4)
                q.wait_ge(SM["s_h3"], b * 4 + nch)
                for qt in range(8):
                    q.dma_start(h_d[qt, :, b * 4:b * 4 + nch, :],
                                hstg[:, b % 2, 0:nch, qt * 128:(qt + 1) * 128]
                                ).then_inc(SM[f"s_hsp{b % 2}"], 16)
            q.wait_ge(SM["b_p3"], 4)
            q.wait_ge(SM["s_hsp0"], 16 * 8 * 11)
            q.wait_ge(SM["s_hsp1"], 16 * 8 * 11)
            q.wait_ge(SM["s_atsp0"], 16 * NH)
            q.wait_ge(SM["s_atsp1"], 16 * NH)
            for blk in range(2):
                if blk == 1:
                    q.wait_ge(SM["s_wdc"], 8 * IC)
                for sl in range(4):
                    q.dma_start(hblk[sl][:, :, :], h_d[blk * 4 + sl, :, :, :]
                                ).then_inc(SM["s_hld"], 16)
                if blk == 1:
                    q.wait_ge(SM["s_woc"], 8 * HC)
                q.dma_start(atblk[:, :, :], at_d[blk, :, :, :]).then_inc(SM["s_atld"], 16)

        @block.gpsimd
        def _(g):
            g.dma_start(I_bf[:], ident[:, :]).then_inc(SM["s_ib"], 16)
            g.dma_start(idf[:], ident[:, :]).then_inc(SM["s_ib"], 16)
            conv = ([("d", i) for i in range(32)] + [("o", i) for i in range(16)])
            def emit_conv(entry):
                kind, i = entry
                if kind == "d":
                    g.dma_start(wdt_bf[i * 344:(i + 1) * 344, :],
                                wdt[i * 344:(i + 1) * 344, :]).then_inc(SM["s_cvt"], 16)
                else:
                    g.dma_start(wo_bf[i * 256:(i + 1) * 256, :],
                                wo[i * 256:(i + 1) * 256, :]).then_inc(SM["s_cvt"], 16)
            for i in range(NT):
                if i >= 2:
                    g.wait_ge(SM["s_xn"], i - 1)
                src = (x_own[i * 128:(i + 1) * 128, :] if i < 8
                       else x_cov[(i - 8) * 128:(i - 7) * 128, :])
                g.dma_start(xtile[:, i % 2, 2048:4096], src[:, 2048:4096]
                            ).then_inc(SM[f"s_x{i % 2}b"], 16)
                if i == 3:
                    for f in range(2):
                        g.dma_start(wkvb[:, f, :, :], wkv4[f, :, :, :]
                                    ).then_inc(SM["s_wkv"], 16)
            g.wait_ge(SM["s_xn"], NT)
            g.nop().then_inc(SM["b_p1"], 1)
            g.wait_ge(SM["b_p1"], 4)
            for h in range(NH):
                if h >= 1:
                    g.wait_ge(SM["s_qpmm"], 2 * h)
                g.dma_start(wqb[:, :, :], wq4[h, :, :, :]).then_inc(SM["s_wq"], 16)
                for _ in range(2):
                    if conv:
                        emit_conv(conv.pop(0))
            while conv:
                emit_conv(conv.pop(0))
            g.wait_ge(SM["s_wq"], 16 * NH)
            g.nop().then_inc(SM["b_p2"], 1)
            g.wait_ge(SM["b_p2"], 4)
            g.dma_start(agnat[0:86, :], agd[:, :]).then_inc(SM["s_agt"], 16)
            g.wait_ge(SM["s_agmm"], 1)
            g.dma_start(agnat[0:86, :], aud[:, :]).then_inc(SM["s_agt"], 16)
            for i in range(IC):
                if i >= 2:
                    g.wait_ge(SM["s_h3"], i - 1)
                g.dma_start(wgb[:, i % 2, :, :], wg4[i, :, :, :]
                            ).then_inc(SM[f"s_wg{i % 2}"], 16)
                g.dma_start(wub[:, i % 2, :, :], wu4[i, :, :, :]
                            ).then_inc(SM[f"s_wu{i % 2}"], 16)
            g.wait_ge(SM["s_wg0"], 16 * 43)
            g.wait_ge(SM["s_wg1"], 16 * 43)
            g.wait_ge(SM["s_wu0"], 16 * 43)
            g.wait_ge(SM["s_wu1"], 16 * 43)
            g.nop().then_inc(SM["b_p3"], 1)
            g.wait_ge(SM["b_p3"], 4)
            g.wait_ge(SM["s_atsp0"], 16 * NH)
            g.wait_ge(SM["s_atsp1"], 16 * NH)
            g.wait_ge(SM["s_hsp0"], 16 * 8 * 11)
            g.wait_ge(SM["s_hsp1"], 16 * 8 * 11)
            g.dma_start(adrow[0:1, :], add_[None, :]).then_inc(SM["s_adr"], 16)
            g.wait_ge(SM["s_cvt"], 16 * 48)
            wdL = 0
            woL = 0
            for oi in range(8):
                blk, o2 = oi // 4, oi % 4
                for j in range((IC + 3) // 4):
                    c = min(4, IC - 4 * j)
                    t0 = 2 * IC * oi + 8 * j
                    if oi > 0 and j < 2:
                        g.wait_ge(SM["s_wdc"], 2 * IC * oi)
                    elif j >= 2:
                        g.wait_ge(SM["s_wdc"], t0 - 8)
                    src = wdt_bf[4 * j * 128:(4 * j + c) * 128,
                                 o2 * 1024:(o2 + 1) * 1024].rearrange(
                                     "(a p) (m n) -> p a m n", p=128, n=512)
                    g.dma_start(wdb[:, 8 * (j % 2):8 * (j % 2) + 2 * c, :]
                                .rearrange("p (a m) n -> p a m n", m=2), src
                                ).then_inc(SM[f"s_wd{wdL % 2}"], 16)
                    wdL += 1
                for jo in range(2):
                    o = 2 * o2 + jo
                    for j in range(HC // 4):
                        t0 = HC * (2 * oi + jo) + 4 * j
                        if t0 >= 8:
                            g.wait_ge(SM["s_woc"], t0 - 4)
                        src = wo_bf[4 * j * 128:(4 * j + 4) * 128,
                                    o * 512:(o + 1) * 512].rearrange(
                                        "(a p) n -> p a n", p=128)
                        g.dma_start(wob[:, 4 * (woL % 2):4 * (woL % 2) + 4, :], src
                                    ).then_inc(SM[f"s_wo{woL % 2}"], 16)
                        woL += 1

        @block.tensor
        def _(t):
            t.wait_ge(SM["s_ib"], 32)
            trg = 0
            for i in range(NT):
                t.wait_ge(SM["s_xn"], i + 1)
                for cc in range(HC):
                    if trg >= 2:
                        t.wait_ge(SM["s_trev"], trg - 1)
                    qd = cc % 4
                    ins = t.matmul(P[:, trg % 2, qd * 128:(qd + 1) * 128],
                                   xntok[:, i % 2, cc * 128:(cc + 1) * 128],
                                   I_bf[:], start=True, stop=True)
                    if qd == 3:
                        ins.then_inc(SM["s_trp"], 1)
                        trg += 1
                if i >= 8 and (i - 8) % 4 == 3:
                    gi = (i - 8) // 4
                    t.wait_ge(SM["s_trev"], trg)
                    t.wait_ge(SM["s_wkv"], 32)
                    if gi >= 1:
                        t.wait_ge(SM["s_kvev"], 3 * gi)
                    for f in range(2):
                        for cc in range(HC):
                            t.ldweights(wkvb[:, f, cc, :])
                            ins = t.matmul(P[:, 2 + f, :], wkvb[:, f, cc, :],
                                           xnTcov[:, cc * 512:(cc + 1) * 512],
                                           start=(cc == 0), stop=(cc == HC - 1))
                        ins.then_inc(SM["s_kv"], 1)
                    t.wait_ge(SM["s_kvev"], 3 * gi + 2)
                    for kc in range(4):
                        ins = t.matmul(P[:, 4, kc * 128:(kc + 1) * 128],
                                       vtstg[:, gi % 2, kc * 128:(kc + 1) * 128],
                                       I_bf[:], start=True, stop=True)
                    ins.then_inc(SM["s_vt"], 1)
                    if gi == NG - 1:
                        t.wait_ge(SM["s_vt"], NG)
                        t.nop().then_inc(SM["b_p1"], 1)
            t.wait_ge(SM["b_p1"], 4)
            pr_cnt = 0
            tail = None
            tail_armed = -10
            cur_si = [0]
            pend = []
            ATT_LAG = 2

            def emit_attnv(d):
                nonlocal tail
                h, X, kt, pr, n, first, last, N, lo, kcb = d
                g = gidx[(h, X)]
                t.wait_ge(SM["s_pt"], n + 1)
                ab = 4 + 2 * (g % 2)
                if first:
                    if g >= 2:
                        t.wait_ge(SM["s_atev"], g - 1)
                        t.wait_ge(SM["s_rzb"], g - 1)
                    if g >= 1:
                        t.wait_ge(SM["s_zev"], g)
                    if g < 2:
                        t.wait_ge(SM["s_qtr"], 4)
                ub = n % 3
                co = lo * 128
                for j in (0, 1):
                    ci = 2 * pr + j
                    t.ldweights(vkm[:, (kcb + ci) * 128:(kcb + ci + 1) * 128])
                    t.matmul(P[:, ab, co:co + N],
                             vkm[:, (kcb + ci) * 128:(kcb + ci + 1) * 128],
                             ptb[:, ub, j, 0:N],
                             start=(first and j == 0), stop=(last and j == 1))
                t.ldweights(ones_bf[:, 0:1])
                for j in (0, 1):
                    ins = t.matmul(P[0:1, 5, co:co + N], ones_bf[:, 0:1],
                                   ptb[:, ub, j, 0:N],
                                   start=(first and j == 0), stop=(last and j == 1))
                ins.then_inc(SM["s_ptfree"], 1)
                if last:
                    tail = (h, X)
                    nonlocal tail_armed
                    tail_armed = cur_si[0]

            def emit_tail_mm():
                nonlocal tail
                th, tX = tail
                tg = gidx[(th, tX)]
                t.wait_ge(SM["s_rz"], tg + 1)
                if tg >= 1:
                    t.wait_ge(SM["s_rzb"], tg)
                ins = t.matmul(P[:, 4 + 2 * (tg % 2), :], ones_bf[0:1, :],
                               rzbf[0:1, tX, :], start=True, stop=True)
                ins.then_inc(SM["s_rzmm"], 1)
                tail = None
                return ins

            def emit_qp_half(j, half):
                t.wait_ge(SM["s_wq"], 16 * (j + 1))
                if j >= 2:
                    t.wait_ge(SM["s_qtr"], 2 * j + half)
                bk = 7 if j >= 2 else (0, 1, 4, 5)[2 * j + half]
                for cc in range(HC):
                    t.ldweights(wqb[:, cc, :])
                    ins = t.matmul(P[:, bk, :], wqb[:, cc, :],
                                   xn[:, cc * 1024 + 512 * half:
                                      cc * 1024 + 512 * half + 512],
                                   start=(cc == 0), stop=(cc == HC - 1))
                ins.then_inc(SM["s_qpmm"], 1)

            for j in range(2):
                for half in range(2):
                    emit_qp_half(j, half)

            for si, stp in enumerate(steps):
                cur_si[0] = si
                if si >= 2:
                    t.wait_ge(SM["s_stepfree"], si - 1)
                pb = si % 2
                _, h, X, kt, pr = stp
                li = si % NSH
                lo, ns = CFG[X][kt]
                N = ns * 128
                qoff = X * 512 + lo * 128
                kcol = X * COV_A + kt * 512
                kcb = X * 32 + kt * 4
                first = (kt == 0 and pr == 0)
                last = (kt == NKT[X] - 1 and pr == 1)
                g = gidx[(h, X)]
                if li == 0:
                    t.wait_ge(SM["s_qtr"], 2 * h + 2)
                for j in (0, 1):
                    ci = 2 * pr + j
                    t.ldweights(ktsb[:, kcol + ci * 128: kcol + (ci + 1) * 128])
                    ins = t.matmul(P[:, 2 * pb + j, 0:N],
                                   ktsb[:, kcol + ci * 128: kcol + (ci + 1) * 128],
                                   qtb[:, h % 2, qoff:qoff + N],
                                   start=True, stop=True)
                ins.then_inc(SM["s_stepdone"], 1)
                if tail is not None and si > tail_armed + 8:
                    emit_tail_mm()
                pend.append((h, X, kt, pr, pr_cnt, first, last, N, lo, kcb))
                pr_cnt += 1
                while len(pend) > ATT_LAG:
                    emit_attnv(pend.pop(0))
                if li == 8 and h + 2 < NH:
                    emit_qp_half(h + 2, 0)
                if li == 22 and h + 2 < NH:
                    emit_qp_half(h + 2, 1)
            while pend:
                emit_attnv(pend.pop(0))
            emit_tail_mm()
            t.wait_ge(SM["s_rzmm"], 2 * NH)
            t.nop().then_inc(SM["b_p2"], 1)
            t.wait_ge(SM["b_p2"], 4)
            t.wait_ge(SM["s_agt"], 16)
            t.matmul(P[:, 0, 0:128], agnat[0:86, :], idf[0:86, :], start=True,
                     stop=True).then_inc(SM["s_agmm"], 1)
            t.wait_ge(SM["s_agt"], 32)
            t.matmul(P[:, 1, 0:128], agnat[0:86, :], idf[0:86, :], start=True,
                     stop=True).then_inc(SM["s_agmm"], 1)
            t.wait_ge(SM["s_agev"], 2)
            for i in range(IC):
                t.wait_ge(SM[f"s_wg{i % 2}"], 16 * (i // 2 + 1))
                t.wait_ge(SM[f"s_wu{i % 2}"], 16 * (i // 2 + 1))
                if i >= 2:
                    t.wait_ge(SM["s_h3"], i - 1)
                pg0, pg1 = (0, 1) if i % 2 == 0 else (4, 5)
                pu0, pu1 = (2, 3) if i % 2 == 0 else (6, 7)
                for cc in range(HC):
                    t.ldweights(wgb[:, i % 2, cc, :])
                    t.matmul(P[:, pg0, :], wgb[:, i % 2, cc, :],
                             xn[:, cc * 1024:cc * 1024 + 512],
                             start=(cc == 0), stop=(cc == HC - 1))
                    t.matmul(P[:, pg1, :], wgb[:, i % 2, cc, :],
                             xn[:, cc * 1024 + 512:(cc + 1) * 1024],
                             start=(cc == 0), stop=(cc == HC - 1))
                for cc in range(HC):
                    t.ldweights(wub[:, i % 2, cc, :])
                    t.matmul(P[:, pu0, :], wub[:, i % 2, cc, :],
                             xn[:, cc * 1024:cc * 1024 + 512],
                             start=(cc == 0), stop=(cc == HC - 1))
                    ins = t.matmul(P[:, pu1, :], wub[:, i % 2, cc, :],
                                   xn[:, cc * 1024 + 512:(cc + 1) * 1024],
                                   start=(cc == 0), stop=(cc == HC - 1))
                ins.then_inc(SM["s_mm3"], 1)
            t.wait_ge(SM["s_mm3"], IC)
            t.nop().then_inc(SM["b_p3"], 1)
            t.wait_ge(SM["b_p3"], 4)
            t.wait_ge(SM["s_adr"], 16)
            for o in range(8):
                if o >= 2:
                    t.wait_ge(SM["s_adb"], o - 1)
                ins = t.matmul(P[:, o % 2, :], onesf[0:1, :],
                               adrow[0:1, o * 512:(o + 1) * 512], start=True, stop=True)
                ins.then_inc(SM["s_adm"], 1)
            for oi in range(8):
                blk, o2 = oi // 4, oi % 4
                if o2 == 0:
                    t.wait_ge(SM["s_hld"], 16 * 4 * (blk + 1))
                    t.wait_ge(SM["s_atld"], 16 * (blk + 1))
                if oi == 0:
                    t.wait_ge(SM["s_adb"], 8)
                else:
                    t.wait_ge(SM["s_gb"], oi * 8)
                    t.wait_ge(SM["s_sgb"], oi * 8)
                for icx in range(IC):
                    L = oi * 22 + icx // 4
                    slot = 8 * ((icx // 4) % 2) + 2 * (icx % 4)
                    t.wait_ge(SM[f"s_wd{L % 2}"], 16 * (L // 2 + 1))
                    for sl in range(4):
                        t.ldweights(hblk[sl][:, icx, :])
                        for j in (0, 1):
                            ins = t.matmul(P[:, sl + 4 * j, :], hblk[sl][:, icx, :],
                                           wdb[:, slot + j, :],
                                           start=(icx == 0), stop=(icx == IC - 1))
                            if icx == IC - 1 and not (sl == 3 and j == 1):
                                ins.then_inc(SM["s_f4"], 1)
                    ins.then_inc(SM["s_wdc"], 2)
                for jo in range(2):
                    t.wait_ge(SM["s_gb"], oi * 8 + 4 * (jo + 1))
                    for hcx in range(HC):
                        L = (2 * oi + jo) * 8 + hcx // 4
                        slot = 4 * (L % 2) + hcx % 4
                        t.wait_ge(SM[f"s_wo{L % 2}"], 16 * (L // 2 + 1))
                        for sl in range(4):
                            t.ldweights(atblk[:, hcx, sl * 128:(sl + 1) * 128])
                            ins = t.matmul(P[:, sl + 4 * jo, :],
                                           atblk[:, hcx, sl * 128:(sl + 1) * 128],
                                           wob[:, slot, :],
                                           start=(hcx == 0), stop=(hcx == HC - 1))
                            if hcx == HC - 1 and sl < 3:
                                ins.then_inc(SM["s_s4"], 1)
                        ins.then_inc(SM["s_woc"], 1)

        @block.vector
        def _(v):
            tv.mark(v.memset(ones_bf[:], 1.0))
            tv.mark(v.memset(onesf[:], 1.0))
            for i in range(NT):
                v.wait_ge(SM[f"s_x{i % 2}"], 16 * (i // 2 + 1))
                tv.wait_self(v)
                for j in range(4):
                    ins = v.bn_stats(bst[:, i % 2, j * 6:(j + 1) * 6],
                                     xtile[:, i % 2, j * 512:(j + 1) * 512])
                tv.mark(ins)
                v.wait_ge(SM[f"s_x{i % 2}b"], 16 * (i // 2 + 1))
                for j in range(4, 8):
                    ins = v.bn_stats(bst[:, i % 2, j * 6:(j + 1) * 6],
                                     xtile[:, i % 2, j * 512:(j + 1) * 512])
                tv.mark(ins)
                tv.wait_self(v)
                ins = v.bn_aggr(mv[:, i % 2, :], bst[:, i % 2, :])
                tv.mark(ins)
                tv.wait_self(v)
                ins = v.tensor_scalar_add(varp[:, i % 2, :], mv[:, i % 2, 1:2], EPS)
                tv.inc(ins, SM, "s_stat", 1)
                v.wait_ge(SM["s_rstd"], i + 1)
                if i >= 2:
                    v.wait_ge(SM["s_trp"], 8 * (i - 1))
                tv.wait_self(v)
                ins = v.tensor_scalar(xntok[:, i % 2, :], xtile[:, i % 2, :],
                                      mv[:, i % 2, 0:1], rstd[:, i % 2, 0:1],
                                      ALU.subtract, ALU.mult)
                tv.inc(ins, SM, "s_xn", 1)
            tv.wait_self(v)
            v.nop().then_inc(SM["b_p1"], 1)
            v.wait_ge(SM["b_p1"], 4)
            v.wait_ge(SM["s_rp"], 16)
            pr_cnt2 = 0
            pend_t = []
            pend_t2 = []

            def emit_tail_v(entry):
                th, tX, lpi = entry
                tg = gidx[(th, tX)]
                if tg >= 2:
                    v.wait_ge(SM[f"s_atsp{tX}"], 16 * (tg // 2))
                v.wait_ge(SM["s_ptfree"], lpi + 1)
                tv.wait_self(v)
                ins = v.tensor_copy(atstg[:, tX, :], P[:, 4 + 2 * (tg % 2), :])
                tv.inc(ins, SM, "s_atev", 1)
                if tg >= 2:
                    v.wait_ge(SM["s_rzmm"], tg - 1)
                tv.wait_self(v)
                ins = v.reciprocal(rzrow[0:1, :], P[0:1, 5, :])
                tv.inc(ins, SM, "s_zev", 1)
                tv.wait_self(v)
                ins = v.tensor_copy(rzbf[0:1, tX, :], rzrow[0:1, :])
                tv.inc(ins, SM, "s_rz", 1)
                pend_t2.append((th, tX, tg))

            def emit_tail_v2(entry):
                th, tX, tg = entry
                v.wait_ge(SM["s_rzmm"], tg + 1)
                tv.wait_self(v)
                ins = v.tensor_copy(rzbsb[:, tX, :], P[:, 4 + 2 * (tg % 2), :])
                tv.inc(ins, SM, "s_rzb", 1)
                th2 = tg // 2
                if th2 >= 1:
                    v.wait_ge(SM[f"s_atsp{tX}"], 16 * th2)
                tv.wait_self(v)
                ins = v.tensor_tensor(atn2[:, tX, :], atstg[:, tX, :],
                                      rzbsb[:, tX, :], ALU.mult)
                tv.inc(ins, SM, "s_atn2", 1)

            def emit_qtr(j, half):
                bk = 7 if j >= 2 else (0, 1, 4, 5)[2 * j + half]
                v.wait_ge(SM["s_qpmm"], 2 * j + half + 1)
                tv.wait_self(v)
                ins = v.tensor_scalar(qtb[:, j % 2, 512 * half:512 * half + 512],
                                      P[:, bk, :], SCALE, None, ALU.mult)
                tv.inc(ins, SM, "s_qtr", 1)

            for j in range(2):
                for half in range(2):
                    emit_qtr(j, half)

            for si, stp in enumerate(steps):
                _, h, X, kt, pr = stp
                li = si % NSH
                if li == 2 and 2 <= h + 1 < NH:
                    emit_qtr(h + 1, 1)
                if li == 16 and h + 2 < NH:
                    emit_qtr(h + 2, 0)
                lo, ns = CFG[X][kt]
                N = ns * 128
                first = (kt == 0 and pr == 0)
                last = (kt == NKT[X] - 1 and pr == 1)
                pb = si % 2
                ub = pr_cnt2 % 3
                o = RP_OFF[(X, kt)] + 2 * pr * N
                v.wait_ge(SM["s_stepdone"], si + 1)
                if pr_cnt2 >= 3:
                    v.wait_ge(SM["s_pt"], pr_cnt2 - 2)
                tv.wait_self(v)
                ins = v.scalar_tensor_tensor(
                    usb[:, ub, :, 0:N],
                    rpsb[:, o:o + 2 * N].rearrange("p (a b) -> p a b", b=N),
                    float(SLOPES[h]), P[:, 2 * pb:2 * pb + 2, 0:N],
                    ALU.mult, ALU.add)
                tv.inc(ins, SM, "s_stepfree", 1)
                pr_cnt2 += 1
                if last:
                    pend_t.append((h, X, pair_idx[(h, X, kt, pr)]))
                while pend_t and pend_t[0][2] + 2 < pr_cnt2:
                    while pend_t2:
                        emit_tail_v2(pend_t2.pop(0))
                    emit_tail_v(pend_t.pop(0))
            while pend_t:
                while pend_t2:
                    emit_tail_v2(pend_t2.pop(0))
                emit_tail_v(pend_t.pop(0))
            while pend_t2:
                emit_tail_v2(pend_t2.pop(0))
            tv.wait_self(v)
            v.nop().then_inc(SM["b_p2"], 1)
            v.wait_ge(SM["b_p2"], 4)
            v.wait_ge(SM["s_agmm"], 1)
            tv.wait_self(v)
            ins = v.tensor_copy(ag_t[:], P[:, 0, 0:128])
            tv.inc(ins, SM, "s_agev", 1)
            v.wait_ge(SM["s_agmm"], 2)
            tv.wait_self(v)
            ins = v.tensor_copy(au_t[:], P[:, 1, 0:128])
            tv.inc(ins, SM, "s_agev", 1)
            for i in range(IC):
                v.wait_ge(SM["s_sil"], i + 1)
                b = i // 4
                bi = b % 2
                slt = i % 4
                if b >= 2:
                    v.wait_ge(SM[f"s_hsp{b % 2}"], 16 * 8 * (b // 2))
                pu0, pu1 = (2, 3) if i % 2 == 0 else (6, 7)
                tv.wait_self(v)
                ins = v.scalar_tensor_tensor(
                    hstg[:, bi, slt, 0:512], P[:, pu0, :], au_t[:, i:i + 1],
                    sgs[:, i % 2, 0, :], ALU.mult, ALU.mult)
                tv.mark(ins)
                ins = v.scalar_tensor_tensor(
                    hstg[:, bi, slt, 512:1024], P[:, pu1, :], au_t[:, i:i + 1],
                    sgs[:, i % 2, 1, :], ALU.mult, ALU.mult)
                tv.inc(ins, SM, "s_h3", 1)
            tv.wait_self(v)
            v.nop().then_inc(SM["b_p3"], 1)
            v.wait_ge(SM["b_p3"], 4)
            for oi in range(8):
                blk, o2 = oi // 4, oi % 4
                for j in range(2):
                    o = 2 * o2 + j
                    for sl in range(4):
                        k4 = (blk * 8 + o) * 4 + sl
                        idx = 2 * sl + j
                        if idx < 7:
                            v.wait_ge(SM["s_f4"], oi * 7 + idx + 1)
                        else:
                            v.wait_ge(SM["s_wdc"], (oi + 1) * 2 * IC)
                        if k4 >= 2:
                            v.wait_ge(SM["s_g2"], k4 - 1)
                        tv.wait_self(v)
                        ins = v.tensor_tensor(gbuf[:, sl % 2, :], P[:, sl + 4 * j, :],
                                              adb[:, o * 512:(o + 1) * 512], ALU.mult)
                        tv.inc(ins, SM, "s_gb", 1)
                    for sl in range(4):
                        k4 = (blk * 8 + o) * 4 + sl
                        if sl < 3:
                            v.wait_ge(SM["s_s4"], oi * 6 + 3 * j + sl + 1)
                        else:
                            v.wait_ge(SM["s_woc"], (2 * oi + j + 1) * HC)
                        v.wait_ge(SM["s_g2"], k4 + 1)
                        tv.wait_self(v)
                        ins = v.scalar_tensor_tensor(sgb[:, sl % 2, :],
                                                     P[:, sl + 4 * j, :],
                                                     1.0, g2buf[:, sl % 2, :],
                                                     ALU.mult, ALU.mult)
                        tv.inc(ins, SM, "s_sgb", 1)
                        v.wait_ge(SM[f"s_xe{k4 % 4}"], 16 * (k4 // 4 + 1))
                        if k4 >= 2:
                            v.wait_ge(SM[f"s_out{k4 % 2}"], 16 * (k4 // 2))
                        tv.wait_self(v)
                        ins = v.tensor_tensor(oeb[:, sl % 2, :], sgb[:, sl % 2, :],
                                              xeb[:, k4 % 4, :], ALU.add)
                        tv.inc(ins, SM, "s_oe", 1)

        @block.scalar
        def _(s):
            trg = 0
            for i in range(NT):
                s.wait_ge(SM["s_stat"], i + 1)
                ta.wait_self(s)
                ins = s.activation(lcol[:, i % 2, :], varp[:, i % 2, :], AF.Ln)
                ta.mark(ins)
                ta.wait_self(s)
                ins = s.activation(rstd[:, i % 2, :], lcol[:, i % 2, :], AF.Exp,
                                   scale=-0.5)
                ta.inc(ins, SM, "s_rstd", 1)
                for gq in range(8):
                    s.wait_ge(SM["s_trp"], trg + 1)
                    bnk = trg % 2
                    bc = gq * 4
                    if i < 8:
                        dst = xnv[:, bc:bc + 4, i * 128:(i + 1) * 128]
                    else:
                        j = (i - 8) % 4
                        dst = xcv[:, bc:bc + 4, j * 128:(j + 1) * 128]
                    ta.wait_self(s)
                    ins = s.activation(dst,
                                       P[:, bnk, :].rearrange("p (c t) -> p c t", t=128),
                                       AF.Copy)
                    ta.inc(ins, SM, "s_trev", 1)
                    trg += 1
                if i >= 8 and (i - 8) % 4 == 3:
                    gi = (i - 8) // 4
                    s.wait_ge(SM["s_kv"], 2 * (gi + 1))
                    ta.wait_self(s)
                    ins = s.activation(ktsb[:, gi * 512:(gi + 1) * 512], P[:, 2, :],
                                       AF.Copy)
                    ta.inc(ins, SM, "s_kvev", 1)
                    if gi >= 2:
                        s.wait_ge(SM["s_vt"], gi - 1)
                    ta.wait_self(s)
                    ins = s.activation(vtstg[:, gi % 2, :], P[:, 3, :], AF.Copy)
                    ta.inc(ins, SM, "s_kvev", 1)
                    s.wait_ge(SM["s_vt"], gi + 1)
                    ta.wait_self(s)
                    ins = s.activation(vkm[:, gi * 512:(gi + 1) * 512], P[:, 4, :],
                                       AF.Copy)
                    ta.inc(ins, SM, "s_kvev", 1)
            ta.wait_self(s)
            s.nop().then_inc(SM["b_p1"], 1)
            s.wait_ge(SM["b_p1"], 4)
            pr_cnt3 = 0
            for si, stp in enumerate(steps):
                if stp[0] != "pr":
                    continue
                _, h, X, kt, pr = stp
                N = CFG[X][kt][1] * 128
                ubp = pr_cnt3 % 3
                ubu = pr_cnt3 % 3
                s.wait_ge(SM["s_stepfree"], si + 1)
                if pr_cnt3 >= 3:
                    s.wait_ge(SM["s_ptfree"], pr_cnt3 - 2)
                ta.wait_self(s)
                ins = s.activation(ptb[:, ubp, :, 0:N], usb[:, ubu, :, 0:N], AF.Exp)
                ta.inc(ins, SM, "s_pt", 1)
                pr_cnt3 += 1
            ta.wait_self(s)
            s.nop().then_inc(SM["b_p2"], 1)
            s.wait_ge(SM["b_p2"], 4)
            for i in range(IC):
                s.wait_ge(SM["s_mm3"], i + 1)
                pg0, pg1 = (0, 1) if i % 2 == 0 else (4, 5)
                if i >= 2:
                    s.wait_ge(SM["s_h3"], i - 1)
                ta.wait_self(s)
                ins = s.activation(sgs[:, i % 2, 0, :], P[:, pg0, :], AF.Silu,
                                   scale=ag_t[:, i:i + 1])
                ta.mark(ins)
                ins = s.activation(sgs[:, i % 2, 1, :], P[:, pg1, :], AF.Silu,
                                   scale=ag_t[:, i:i + 1])
                ta.inc(ins, SM, "s_sil", 1)
            ta.wait_self(s)
            s.nop().then_inc(SM["b_p3"], 1)
            s.wait_ge(SM["b_p3"], 4)
            s.wait_ge(SM["s_hsp0"], 16 * 8 * 11)
            s.wait_ge(SM["s_hsp1"], 16 * 8 * 11)
            for o in range(8):
                s.wait_ge(SM["s_adm"], o + 1)
                ta.wait_self(s)
                ins = s.activation(adb[:, o * 512:(o + 1) * 512], P[:, o % 2, :],
                                   AF.Copy)
                ta.inc(ins, SM, "s_adb", 1)
            for blk in range(2):
                for o in range(8):
                    for sl in range(4):
                        k4 = (blk * 8 + o) * 4 + sl
                        if k4 >= 4:
                            s.wait_ge(SM["s_oe"], k4 - 3)
                        s.dma_start(xeb[:, k4 % 4, :],
                                    x_own[(blk * 4 + sl) * 128:(blk * 4 + sl + 1) * 128,
                                          o * 512:(o + 1) * 512]
                                    ).then_inc(SM[f"s_xe{k4 % 4}"], 16)
                    for sl in range(4):
                        k4 = (blk * 8 + o) * 4 + sl
                        s.wait_ge(SM["s_gb"], k4 + 1)
                        if k4 >= 2:
                            s.wait_ge(SM["s_sgb"], k4 - 1)
                        ta.wait_self(s)
                        ins = s.activation(g2buf[:, sl % 2, :], gbuf[:, sl % 2, :],
                                           AF.Silu)
                        ta.inc(ins, SM, "s_g2", 1)
                    for sl in range(4):
                        k4 = (blk * 8 + o) * 4 + sl
                        s.wait_ge(SM["s_oe"], k4 + 1)
                        s.dma_start(out[(blk * 4 + sl) * 128:(blk * 4 + sl + 1) * 128,
                                        o * 512:(o + 1) * 512],
                                    oeb[:, sl % 2, :]).then_inc(SM[f"s_out{k4 % 2}"], 16)
            s.wait_ge(SM["s_out0"], 16 * 32)
            s.wait_ge(SM["s_out1"], 16 * 32)

    est.close()
    return nc


def _build_rp(p):
    rp = np.empty((128, RP_TOT), np.float32)
    for X in (0, 1):
        vs = [SLOT_V[s] for s in range(4 * X, 4 * X + 4)]
        for kt in range(NKT[X]):
            lo, ns = CFG[X][kt]
            N = ns * 128
            o = RP_OFF[(X, kt)]
            qg = np.concatenate([
                128 * (4 * (vs[lo + j] - 1) + p) + np.arange(128) for j in range(ns)])
            for ci in range(4):
                kg = kt * 512 + ci * 128 + np.arange(128)
                rel = (kg[:, None] - qg[None, :]).astype(np.float32)
                rp[:, o + ci * N:o + (ci + 1) * N] = np.where(rel > 0, NEG, rel)
    return rp


def _stage(inputs):
    x = np.asarray(inputs["L0_attn_hidden_states"])
    Wqkv = np.asarray(inputs["L0_attn_Wqkv"])
    Wo = np.asarray(inputs["L0_attn_Wo"])
    Wg = np.asarray(inputs["L0_ffn_Wg"])
    Wu = np.asarray(inputs["L0_ffn_Wu"])
    Wd = np.asarray(inputs["L0_ffn_Wd"])
    ag = np.asarray(inputs["L0_ffn_ag"])
    au = np.asarray(inputs["L0_ffn_au"])
    ad = np.asarray(inputs["L0_ffn_ad"])

    wq4 = np.ascontiguousarray(
        Wqkv[:, :HID].reshape(HC, 128, NH, 128).transpose(2, 1, 0, 3))
    wkv4 = np.ascontiguousarray(
        Wqkv[:, HID:].reshape(HC, 128, 2, 128).transpose(2, 1, 0, 3))
    wg4 = np.ascontiguousarray(Wg.reshape(IC, 128, HC, 128).transpose(0, 3, 2, 1))
    wu4 = np.ascontiguousarray(Wu.reshape(IC, 128, HC, 128).transpose(0, 3, 2, 1))
    wdt = np.ascontiguousarray(Wd.T)
    agd = np.ascontiguousarray(ag.reshape(IC, 128))
    aud = np.ascontiguousarray(au.reshape(IC, 128))
    ident = np.eye(128, dtype=np.float32)

    covs = [np.ascontiguousarray(
        np.concatenate([x[A, :COV_A], x[1 - A, :COV_B]], axis=0)) for A in (0, 1)]
    rp_cache = {}
    in_maps, meta = [], []
    for c in range(8):
        A, p = c // 4, c % 4
        slots = []
        for s_ in range(8):
            v = SLOT_V[s_]
            b = A if s_ < 4 else 1 - A
            slots.append((b, 4 * (v - 1) + p))
        x_own = np.ascontiguousarray(
            np.concatenate([x[b, 128 * t:128 * (t + 1)] for (b, t) in slots], axis=0))
        if p not in rp_cache:
            rp_cache[p] = _build_rp(p)
        in_maps.append({
            "x_own": x_own, "x_cov": covs[A], "rp": rp_cache[p],
            "wq4": wq4, "wkv4": wkv4, "wg4": wg4, "wu4": wu4, "wdt": wdt,
            "wo": Wo, "agd": agd, "aud": aud, "add_": ad, "ident": ident,
        })
        meta.append(slots)
    return in_maps, meta


def kernel(**inputs):
    in_maps, meta = _stage(inputs)
    nc = build_nc(dbg=False)
    res = run_bass_kernel_spmd(nc, in_maps, core_ids=list(range(8)))
    y = np.empty((B, S, HID), np.float32)
    for c in range(8):
        oc = res.results[c]["out"]
        for s_, (b, t) in enumerate(meta[c]):
            y[b, 128 * t:128 * (t + 1)] = oc[128 * s_:128 * (s_ + 1)]
    return y


# version 11
# speedup vs baseline: 1.1474x; 1.1474x over previous
import math
from contextlib import ExitStack

import numpy as np

import concourse.bass as bass
import concourse.mybir as mybir
from concourse.bass_utils import run_bass_kernel_spmd

F32 = mybir.dt.float32
BF16 = mybir.dt.bfloat16
AF = mybir.ActivationFunctionType
ALU = mybir.AluOpType

B, S, HID, NH, DH, INTER = 2, 4096, 4096, 32, 128, 11008
HC = HID // 128
IC = INTER // 128
NLD = (IC + 3) // 4
SLOT_V = (2, 4, 6, 8, 1, 3, 5, 7)
COV_A, COV_B = 4096, 3584
NCOV = COV_A + COV_B
NKT = (8, 7)
SCALE = 1.0 / math.sqrt(DH)
EPS = 1e-5
NEG = -1e30
NT = 68
NG = 15


def _alibi_slopes(n=NH):
    cp2 = 2 ** math.floor(math.log2(n))
    base = 2.0 ** (-(2.0 ** (-(math.log2(cp2) - 3))))
    slopes = base ** np.arange(1, cp2 + 1, dtype=np.float64)
    if cp2 != n:
        extra_base = 2.0 ** (-(2.0 ** (-(math.log2(2 * cp2) - 3))))
        rem = min(cp2, n - cp2)
        extra = extra_base ** np.arange(1, 1 + 2 * rem, 2, dtype=np.float64)
        slopes = np.concatenate([slopes, extra])
    return slopes.astype(np.float64)


SLOPES = _alibi_slopes()


def _att_cfg():
    cfg = []
    for X in (0, 1):
        vs = [SLOT_V[s] for s in range(4 * X, 4 * X + 4)]
        cfg.append([([j for j, v in enumerate(vs) if v >= kt + 1][0],
                     len([j for j, v in enumerate(vs) if v >= kt + 1]))
                    for kt in range(NKT[X])])
    return cfg


CFG = _att_cfg()

RP_OFF = {}
_o = 0
for _X in (0, 1):
    for _kt in range(NKT[_X]):
        RP_OFF[(_X, _kt)] = _o
        _o += 4 * CFG[_X][_kt][1] * 128
RP_TOT = _o

M = {}
_c = 16512
def _put(name, nbytes):
    global _c
    M[name] = _c
    _c = (_c + nbytes + 31) // 32 * 32

_put("ones_bf", 256); _put("I_bf", 256); _put("idf", 512); _put("onesf", 512)
_put("ag_t", 512); _put("au_t", 512); _put("agnat", 512)
_put("bst", 384); _put("mv", 64); _put("varp", 32); _put("lcol", 32); _put("rstd", 32)
_put("rzbf", 2048); _put("rzrow", 2048)
assert _c <= 24704, _c
M["xn"] = 24704
M["xtile"] = 90240
M["xntok"] = 123008
M["xnTcov"] = 139392
M["wkvb"] = 172160
M["vtstg"] = 188544
M["ktsb"] = 198624
M["vkm"] = 213984
M["rpsb"] = 90240
M["wqb"] = 163968
M["qtb"] = 172160
M["usb"] = 176256
M["ptb"] = 188544
M["atstg"] = 194688
M["rzbsb"] = M["ag_t"]
assert M["atstg"] + 2048 <= 198624
M["wgb"] = 90240
M["wub"] = 106656
M["sgs"] = 123072
M["hstg"] = 139488
assert 139488 + 16384 <= 229344
M["hblk"] = 24704
M["atblk"] = 112800
M["adb"] = 145600
M["adrow"] = 162016
M["wdb"] = 178432
M["wob"] = 194848
M["gbuf"] = 203072
M["g2buf"] = 207200
M["xeb"] = 211328
M["oeb"] = 219552
M["sgb"] = 223680
assert 223680 + 4096 <= 229344


class ETrack:

    def __init__(self, stack, nc, name, semc):
        self.h = stack.enter_context(nc.semaphore(name))
        self.n = 0
        self.last = None
        self.semc = semc

    def inc(self, ins, sm, name, k=1):
        ins.then_inc(sm[name], k)
        self.semc[name] = self.semc.get(name, 0) + k
        self.last = (sm[name], self.semc[name])
        return ins

    def mark(self, ins):
        self.n += 1
        ins.then_inc(self.h, 1)
        self.last = (self.h, self.n)
        return ins

    def wait_self(self, eng):
        if self.last is not None:
            eng.wait_ge(self.last[0], self.last[1])


def build_nc(dbg=False):
    nc = bass.Bass("TRN2", target_bir_lowering=False)

    x_own = nc.dram_tensor("x_own", [1024, HID], F32, kind="ExternalInput").ap()
    x_cov = nc.dram_tensor("x_cov", [NCOV, HID], F32, kind="ExternalInput").ap()
    wq4 = nc.dram_tensor("wq4", [NH, 128, HC, 128], F32, kind="ExternalInput").ap()
    wkv4 = nc.dram_tensor("wkv4", [2, 128, HC, 128], F32, kind="ExternalInput").ap()
    wg4 = nc.dram_tensor("wg4", [IC, 128, HC, 128], F32, kind="ExternalInput").ap()
    wu4 = nc.dram_tensor("wu4", [IC, 128, HC, 128], F32, kind="ExternalInput").ap()
    wdt = nc.dram_tensor("wdt", [INTER, HID], F32, kind="ExternalInput").ap()
    wo = nc.dram_tensor("wo", [HID, HID], F32, kind="ExternalInput").ap()
    agd = nc.dram_tensor("agd", [IC, 128], F32, kind="ExternalInput").ap()
    aud = nc.dram_tensor("aud", [IC, 128], F32, kind="ExternalInput").ap()
    add_ = nc.dram_tensor("add_", [HID], F32, kind="ExternalInput").ap()
    rp = nc.dram_tensor("rp", [128, RP_TOT], F32, kind="ExternalInput").ap()
    ident = nc.dram_tensor("ident", [128, 128], F32, kind="ExternalInput").ap()
    out = nc.dram_tensor("out", [1024, HID], F32, kind="ExternalOutput").ap()
    ikind = "ExternalOutput" if dbg else "Internal"
    wdt_bf = nc.dram_tensor("wdt_bf", [INTER, HID], BF16, kind="Internal").ap()
    wo_bf = nc.dram_tensor("wo_bf", [HID, HID], BF16, kind="Internal").ap()
    h_d = nc.dram_tensor("h_d", [8, 128, IC, 128], BF16, kind=ikind).ap()
    at_d = nc.dram_tensor("at_d", [2, 128, NH, 512], BF16, kind=ikind).ap()
    if dbg:
        xn_d = nc.dram_tensor("xn_d", [128, HC * 1024], BF16, kind="ExternalOutput").ap()
        kt_d = nc.dram_tensor("kt_d", [128, NCOV], BF16, kind="ExternalOutput").ap()
        vk_d = nc.dram_tensor("vk_d", [128, NCOV], BF16, kind="ExternalOutput").ap()

    def sb(name, shape, dt):
        return nc.alloc_sbuf_tensor_at(name, shape, dt, offset=M[name]).ap()

    ones_bf = sb("ones_bf", [128, 128], BF16)
    I_bf = sb("I_bf", [128, 128], BF16)
    idf = sb("idf", [128, 128], F32)
    onesf = sb("onesf", [128, 128], F32)
    ag_t = sb("ag_t", [128, 128], F32)
    au_t = sb("au_t", [128, 128], F32)
    agnat = sb("agnat", [128, 128], F32)
    bst = sb("bst", [128, 2, 48], F32)
    mv = sb("mv", [128, 2, 2], F32)
    varp = sb("varp", [128, 2, 1], F32)
    lcol = sb("lcol", [128, 2, 1], F32)
    rstd = sb("rstd", [128, 2, 1], F32)
    rzrow = sb("rzrow", [1, 512], F32)
    rzbf = sb("rzbf", [1, 2, 512], BF16)
    xn = sb("xn", [128, HC * 1024], BF16)
    xtile = sb("xtile", [128, 2, HID], F32)
    xntok = sb("xntok", [128, 2, HID], BF16)
    xnTcov = sb("xnTcov", [128, HC * 512], BF16)
    wkvb = sb("wkvb", [128, 2, HC, 128], BF16)
    vtstg = sb("vtstg", [128, 2, 512], BF16)
    ktsb = sb("ktsb", [128, NCOV], BF16)
    vkm = sb("vkm", [128, NCOV], BF16)
    rpsb = sb("rpsb", [128, RP_TOT], F32)
    wqb = sb("wqb", [128, HC, 128], BF16)
    qtb = sb("qtb", [128, 2, 1024], BF16)
    usb = sb("usb", [128, 3, 2, 512], F32)
    ptb = sb("ptb", [128, 3, 2, 512], BF16)
    atstg = sb("atstg", [128, 2, 512], BF16)
    rzbsb = sb("rzbsb", [128, 2, 512], BF16)
    atn2 = atstg
    wgb = sb("wgb", [128, 2, HC, 128], BF16)
    wub = sb("wub", [128, 2, HC, 128], BF16)
    sgs = sb("sgs", [128, 2, 2, 512], F32)
    hstg = sb("hstg", [128, 2, 4, 1024], BF16)
    hblk = [nc.alloc_sbuf_tensor_at(f"hblk{i}", [128, IC, 128], BF16,
                                    offset=M["hblk"] + i * 22016).ap()
            for i in range(4)]
    atblk = sb("atblk", [128, NH, 512], BF16)
    adb = sb("adb", [128, HID], F32)
    adrow = sb("adrow", [1, HID], F32)
    wdb = sb("wdb", [128, 16, 512], BF16)
    wob = sb("wob", [128, 8, 512], BF16)
    gbuf = sb("gbuf", [128, 2, 512], F32)
    g2buf = sb("g2buf", [128, 2, 512], F32)
    xeb = sb("xeb", [128, 4, 512], F32)
    oeb = sb("oeb", [128, 2, 512], F32)
    sgb = sb("sgb", [128, 2, 512], F32)

    P = nc.alloc_psum_tensor("P", [128, 8, 512], F32).ap()
    xnv = xn.rearrange("p (c t) -> p c t", t=1024)
    xcv = xnTcov.rearrange("p (c t) -> p c t", t=512)

    est = ExitStack()
    SM = {}
    semnames = ["s_x0", "s_x1", "s_x0b", "s_x1b", "s_stat", "s_rstd", "s_xn", "s_trp", "s_trev",
                "s_kv", "s_kvev", "s_vt", "s_wkv", "s_ib",
                "b_p1", "s_rp", "s_wq", "s_stepdone", "s_stepfree", "s_qpmm", "s_qtr",
                "s_pt", "s_ptfree", "s_zev", "s_rz", "s_rzmm",
                "s_rzb", "s_atev", "s_atn2", "s_atsp0", "s_atsp1",
                "b_p2", "s_agt", "s_agmm", "s_agev", "s_wg0", "s_wg1", "s_wu0",
                "s_wu1", "s_mm3", "s_sil", "s_h3", "s_hsp0", "s_hsp1",
                "b_p3", "s_hld", "s_atld", "s_adr", "s_adm", "s_adb",
                "s_wdc", "s_wdc2", "s_woc", "s_dbgd", "s_cvt",
                "s_f4", "s_s4", "s_gb", "s_g2", "s_sgb", "s_xe0", "s_xe1", "s_xe2", "s_xe3",
                "s_oe", "s_out0", "s_out1"]
    semnames += [f"s_wd{j}" for j in range(8)] + [f"s_wo{j}" for j in range(8)]
    for s_ in semnames:
        SM[s_] = est.enter_context(nc.semaphore(s_))

    SEMC = {}
    tv = ETrack(est, nc, "ss_v", SEMC)
    ta = ETrack(est, nc, "ss_a", SEMC)
    tp = ETrack(est, nc, "ss_p", SEMC)

    steps = []
    for h in range(NH):
        for X in (0, 1):
            for kt in range(NKT[X]):
                for pr in (0, 1):
                    steps.append(("pr", h, X, kt, pr))
    NSTEP = len(steps)
    NSH = 30
    pair_idx = {}
    _pi = 0
    for stp in steps:
        if stp[0] == "pr":
            pair_idx[stp[1:]] = _pi
            _pi += 1
    gidx = {}
    _gi = 0
    for h in range(NH):
        for X in (0, 1):
            gidx[(h, X)] = _gi
            _gi += 1

    with nc.Block() as block:

        @block.sync
        def _(q):
            for i in range(NT):
                if i >= 2:
                    q.wait_ge(SM["s_xn"], i - 1)
                src = (x_own[i * 128:(i + 1) * 128, :] if i < 8
                       else x_cov[(i - 8) * 128:(i - 7) * 128, :])
                q.dma_start(xtile[:, i % 2, 0:2048], src[:, 0:2048]
                            ).then_inc(SM[f"s_x{i % 2}"], 16)
            q.wait_ge(SM["b_p1"], 4)
            for ck in range(4):
                q.dma_start(rpsb[:, ck * 4608:(ck + 1) * 4608],
                            rp[:, ck * 4608:(ck + 1) * 4608]
                            ).then_inc(SM[f"s_wd{4 + ck}"], 16)
            for h in range(NH):
                for X in (0, 1):
                    g = gidx[(h, X)]
                    q.wait_ge(SM["s_atn2"], g + 1)
                    q.dma_start(at_d[X, :, h, :], atn2[:, X, :]
                                ).then_inc(SM[f"s_atsp{X}"], 16)
            if dbg:
                q.wait_ge(SM["b_p2"], 4)
                q.dma_start(xn_d[:, :], xn[:]).then_inc(SM["s_dbgd"], 16)
                q.dma_start(kt_d[:, :], ktsb[:]).then_inc(SM["s_dbgd"], 16)
                q.dma_start(vk_d[:, :], vkm[:]).then_inc(SM["s_dbgd"], 16)
                q.wait_ge(SM["s_dbgd"], 48)
            nbatch = (IC + 3) // 4
            for b in range(nbatch):
                nch = min(4, IC - b * 4)
                q.wait_ge(SM["s_h3"], b * 4 + nch)
                for qt in range(8):
                    q.dma_start(h_d[qt, :, b * 4:b * 4 + nch, :],
                                hstg[:, b % 2, 0:nch, qt * 128:(qt + 1) * 128]
                                ).then_inc(SM[f"s_hsp{b % 2}"], 16)
            q.wait_ge(SM["b_p3"], 4)
            q.wait_ge(SM["s_hsp0"], 16 * 8 * 11)
            q.wait_ge(SM["s_hsp1"], 16 * 8 * 11)
            q.wait_ge(SM["s_atsp0"], 16 * NH)
            q.wait_ge(SM["s_atsp1"], 16 * NH)
            for blk in range(2):
                if blk == 1:
                    q.wait_ge(SM["s_wdc"], 4 * IC)
                    q.wait_ge(SM["s_wdc2"], 4 * IC)
                for sl in range(4):
                    q.dma_start(hblk[sl][:, :, :], h_d[blk * 4 + sl, :, :, :]
                                ).then_inc(SM["s_hld"], 16)
                if blk == 1:
                    q.wait_ge(SM["s_woc"], 8 * HC)
                q.dma_start(atblk[:, :, :], at_d[blk, :, :, :]).then_inc(SM["s_atld"], 16)

        @block.gpsimd
        def _(g):
            g.dma_start(I_bf[:], ident[:, :]).then_inc(SM["s_ib"], 16)
            g.dma_start(idf[:], ident[:, :]).then_inc(SM["s_ib"], 16)
            conv = ([("d", i) for i in range(32)] + [("o", i) for i in range(16)])
            def emit_conv(entry):
                kind, i = entry
                if kind == "d":
                    g.dma_start(wdt_bf[i * 344:(i + 1) * 344, :],
                                wdt[i * 344:(i + 1) * 344, :]).then_inc(SM["s_cvt"], 16)
                else:
                    g.dma_start(wo_bf[i * 256:(i + 1) * 256, :],
                                wo[i * 256:(i + 1) * 256, :]).then_inc(SM["s_cvt"], 16)
            for i in range(NT):
                if i >= 2:
                    g.wait_ge(SM["s_xn"], i - 1)
                src = (x_own[i * 128:(i + 1) * 128, :] if i < 8
                       else x_cov[(i - 8) * 128:(i - 7) * 128, :])
                g.dma_start(xtile[:, i % 2, 2048:4096], src[:, 2048:4096]
                            ).then_inc(SM[f"s_x{i % 2}b"], 16)
                if i == 3:
                    for f in range(2):
                        g.dma_start(wkvb[:, f, :, :], wkv4[f, :, :, :]
                                    ).then_inc(SM["s_wkv"], 16)
            g.wait_ge(SM["s_kv"], 2 * NG)
            g.dma_start(wqb[:, :, :], wq4[0, :, :, :]).then_inc(SM["s_wq"], 16)
            g.wait_ge(SM["s_xn"], NT)
            g.nop().then_inc(SM["b_p1"], 1)
            g.wait_ge(SM["b_p1"], 4)
            for h in range(1, NH):
                g.wait_ge(SM["s_qpmm"], 2 * h)
                g.dma_start(wqb[:, :, :], wq4[h, :, :, :]).then_inc(SM["s_wq"], 16)
                for _ in range(2):
                    if conv:
                        emit_conv(conv.pop(0))
            while conv:
                emit_conv(conv.pop(0))
            g.wait_ge(SM["s_wq"], 16 * NH)
            g.nop().then_inc(SM["b_p2"], 1)
            g.wait_ge(SM["b_p2"], 4)
            g.dma_start(agnat[0:86, :], agd[:, :]).then_inc(SM["s_agt"], 16)
            g.wait_ge(SM["s_agmm"], 1)
            g.dma_start(agnat[0:86, :], aud[:, :]).then_inc(SM["s_agt"], 16)
            for i in range(IC):
                if i >= 2:
                    g.wait_ge(SM["s_h3"], i - 1)
                g.dma_start(wgb[:, i % 2, :, :], wg4[i, :, :, :]
                            ).then_inc(SM[f"s_wg{i % 2}"], 16)
                g.dma_start(wub[:, i % 2, :, :], wu4[i, :, :, :]
                            ).then_inc(SM[f"s_wu{i % 2}"], 16)
            g.wait_ge(SM["s_wg0"], 16 * 43)
            g.wait_ge(SM["s_wg1"], 16 * 43)
            g.wait_ge(SM["s_wu0"], 16 * 43)
            g.wait_ge(SM["s_wu1"], 16 * 43)
            g.nop().then_inc(SM["b_p3"], 1)
            g.wait_ge(SM["b_p3"], 4)
            g.wait_ge(SM["s_atsp0"], 16 * NH)
            g.wait_ge(SM["s_atsp1"], 16 * NH)
            g.wait_ge(SM["s_hsp0"], 16 * 8 * 11)
            g.wait_ge(SM["s_hsp1"], 16 * 8 * 11)
            g.dma_start(adrow[0:1, :], add_[None, :]).then_inc(SM["s_adr"], 16)
            g.wait_ge(SM["s_cvt"], 16 * 48)
            woL = 0
            def _endicx(L):
                return (L // NLD) * IC + min(4 * (L % NLD) + 4, IC)
            for oi in range(8):
                blk, o2 = oi // 4, oi % 4
                for m in range(2):
                    cons = "s_wdc" if m == 0 else "s_wdc2"
                    for jj in range(NLD):
                        c = min(4, IC - 4 * jj)
                        L = oi * NLD + jj
                        if L >= 2:
                            g.wait_ge(SM[cons], _endicx(L - 2))
                        src = wdt_bf[4 * jj * 128:(4 * jj + c) * 128,
                                     o2 * 1024 + 512 * m:o2 * 1024 + 512 * m + 512
                                     ].rearrange("(a p) n -> p a n", p=128)
                        g.dma_start(wdb[:, 8 * m + 4 * (L % 2):
                                        8 * m + 4 * (L % 2) + c, :], src
                                    ).then_inc(SM[f"s_wd{2 * m + L % 2}"], 16)
                for jo in range(2):
                    o = 2 * o2 + jo
                    for j in range(HC // 4):
                        t0 = HC * (2 * oi + jo) + 4 * j
                        if t0 >= 8:
                            g.wait_ge(SM["s_woc"], t0 - 4)
                        src = wo_bf[4 * j * 128:(4 * j + 4) * 128,
                                    o * 512:(o + 1) * 512].rearrange(
                                        "(a p) n -> p a n", p=128)
                        g.dma_start(wob[:, 4 * (woL % 2):4 * (woL % 2) + 4, :], src
                                    ).then_inc(SM[f"s_wo{woL % 2}"], 16)
                        woL += 1

        @block.tensor
        def _(t):
            t.wait_ge(SM["s_ib"], 32)
            trg = 0
            for i in range(NT):
                t.wait_ge(SM["s_xn"], i + 1)
                for cc in range(HC):
                    if trg >= 2:
                        t.wait_ge(SM["s_trev"], trg - 1)
                    qd = cc % 4
                    ins = t.matmul(P[:, trg % 2, qd * 128:(qd + 1) * 128],
                                   xntok[:, i % 2, cc * 128:(cc + 1) * 128],
                                   I_bf[:], start=True, stop=True)
                    if qd == 3:
                        ins.then_inc(SM["s_trp"], 1)
                        trg += 1
                if i >= 8 and (i - 8) % 4 == 3:
                    gi = (i - 8) // 4
                    t.wait_ge(SM["s_trev"], trg)
                    t.wait_ge(SM["s_wkv"], 32)
                    if gi >= 1:
                        t.wait_ge(SM["s_kvev"], 3 * gi)
                    for f in range(2):
                        for cc in range(HC):
                            t.ldweights(wkvb[:, f, cc, :])
                            ins = t.matmul(P[:, 2 + f, :], wkvb[:, f, cc, :],
                                           xnTcov[:, cc * 512:(cc + 1) * 512],
                                           start=(cc == 0), stop=(cc == HC - 1))
                        ins.then_inc(SM["s_kv"], 1)
                    t.wait_ge(SM["s_kvev"], 3 * gi + 2)
                    for kc in range(4):
                        ins = t.matmul(P[:, 4, kc * 128:(kc + 1) * 128],
                                       vtstg[:, gi % 2, kc * 128:(kc + 1) * 128],
                                       I_bf[:], start=True, stop=True)
                    ins.then_inc(SM["s_vt"], 1)
                    if gi == NG - 1:
                        t.wait_ge(SM["s_vt"], NG)
                        t.nop().then_inc(SM["b_p1"], 1)
            t.wait_ge(SM["b_p1"], 4)
            pr_cnt = 0
            tail = None
            tail_armed = -10
            cur_si = [0]
            pend = []
            ATT_LAG = 2

            def emit_attnv(d):
                nonlocal tail
                h, X, kt, pr, n, first, last, N, lo, kcb = d
                g = gidx[(h, X)]
                t.wait_ge(SM["s_pt"], n + 1)
                ab = 4 + 2 * (g % 2)
                if first:
                    if g >= 2:
                        t.wait_ge(SM["s_atev"], g - 1)
                        t.wait_ge(SM["s_rzb"], g - 1)
                    if g >= 1:
                        t.wait_ge(SM["s_zev"], g)
                    if g < 2:
                        t.wait_ge(SM["s_qtr"], 4)
                ub = n % 3
                co = lo * 128
                for j in (0, 1):
                    ci = 2 * pr + j
                    t.ldweights(vkm[:, (kcb + ci) * 128:(kcb + ci + 1) * 128])
                    t.matmul(P[:, ab, co:co + N],
                             vkm[:, (kcb + ci) * 128:(kcb + ci + 1) * 128],
                             ptb[:, ub, j, 0:N],
                             start=(first and j == 0), stop=(last and j == 1))
                t.ldweights(ones_bf[:, 0:1])
                for j in (0, 1):
                    ins = t.matmul(P[0:1, 5, co:co + N], ones_bf[:, 0:1],
                                   ptb[:, ub, j, 0:N],
                                   start=(first and j == 0), stop=(last and j == 1))
                ins.then_inc(SM["s_ptfree"], 1)
                if last:
                    tail = (h, X)
                    nonlocal tail_armed
                    tail_armed = cur_si[0]

            def emit_tail_mm():
                nonlocal tail
                th, tX = tail
                tg = gidx[(th, tX)]
                t.wait_ge(SM["s_rz"], tg + 1)
                if tg >= 1:
                    t.wait_ge(SM["s_rzb"], tg)
                ins = t.matmul(P[:, 4 + 2 * (tg % 2), :], ones_bf[0:1, :],
                               rzbf[0:1, tX, :], start=True, stop=True)
                ins.then_inc(SM["s_rzmm"], 1)
                tail = None
                return ins

            def emit_qp_half(j, half):
                t.wait_ge(SM["s_wq"], 16 * (j + 1))
                if j >= 2:
                    t.wait_ge(SM["s_qtr"], 2 * j + half)
                bk = 7 if j >= 2 else (0, 1, 4, 5)[2 * j + half]
                for cc in range(HC):
                    t.ldweights(wqb[:, cc, :])
                    ins = t.matmul(P[:, bk, :], wqb[:, cc, :],
                                   xn[:, cc * 1024 + 512 * half:
                                      cc * 1024 + 512 * half + 512],
                                   start=(cc == 0), stop=(cc == HC - 1))
                ins.then_inc(SM["s_qpmm"], 1)

            for j in range(2):
                for half in range(2):
                    emit_qp_half(j, half)

            for si, stp in enumerate(steps):
                cur_si[0] = si
                if si >= 2:
                    t.wait_ge(SM["s_stepfree"], si - 1)
                pb = si % 2
                _, h, X, kt, pr = stp
                li = si % NSH
                lo, ns = CFG[X][kt]
                N = ns * 128
                qoff = X * 512 + lo * 128
                kcol = X * COV_A + kt * 512
                kcb = X * 32 + kt * 4
                first = (kt == 0 and pr == 0)
                last = (kt == NKT[X] - 1 and pr == 1)
                g = gidx[(h, X)]
                if li == 0:
                    t.wait_ge(SM["s_qtr"], 2 * h + 2)
                for j in (0, 1):
                    ci = 2 * pr + j
                    t.ldweights(ktsb[:, kcol + ci * 128: kcol + (ci + 1) * 128])
                    ins = t.matmul(P[:, 2 * pb + j, 0:N],
                                   ktsb[:, kcol + ci * 128: kcol + (ci + 1) * 128],
                                   qtb[:, h % 2, qoff:qoff + N],
                                   start=True, stop=True)
                ins.then_inc(SM["s_stepdone"], 1)
                if tail is not None and si > tail_armed + 8:
                    emit_tail_mm()
                pend.append((h, X, kt, pr, pr_cnt, first, last, N, lo, kcb))
                pr_cnt += 1
                while len(pend) > ATT_LAG:
                    emit_attnv(pend.pop(0))
                if li == 8 and h + 2 < NH:
                    emit_qp_half(h + 2, 0)
                if li == 22 and h + 2 < NH:
                    emit_qp_half(h + 2, 1)
            while pend:
                emit_attnv(pend.pop(0))
            emit_tail_mm()
            t.wait_ge(SM["s_rzmm"], 2 * NH)
            t.nop().then_inc(SM["b_p2"], 1)
            t.wait_ge(SM["b_p2"], 4)
            t.wait_ge(SM["s_agt"], 16)
            t.matmul(P[:, 0, 0:128], agnat[0:86, :], idf[0:86, :], start=True,
                     stop=True).then_inc(SM["s_agmm"], 1)
            t.wait_ge(SM["s_agt"], 32)
            t.matmul(P[:, 1, 0:128], agnat[0:86, :], idf[0:86, :], start=True,
                     stop=True).then_inc(SM["s_agmm"], 1)
            t.wait_ge(SM["s_agev"], 2)
            for i in range(IC):
                t.wait_ge(SM[f"s_wg{i % 2}"], 16 * (i // 2 + 1))
                t.wait_ge(SM[f"s_wu{i % 2}"], 16 * (i // 2 + 1))
                if i >= 2:
                    t.wait_ge(SM["s_h3"], i - 1)
                pg0, pg1 = (0, 1) if i % 2 == 0 else (4, 5)
                pu0, pu1 = (2, 3) if i % 2 == 0 else (6, 7)
                for cc in range(HC):
                    t.ldweights(wgb[:, i % 2, cc, :])
                    t.matmul(P[:, pg0, :], wgb[:, i % 2, cc, :],
                             xn[:, cc * 1024:cc * 1024 + 512],
                             start=(cc == 0), stop=(cc == HC - 1))
                    t.matmul(P[:, pg1, :], wgb[:, i % 2, cc, :],
                             xn[:, cc * 1024 + 512:(cc + 1) * 1024],
                             start=(cc == 0), stop=(cc == HC - 1))
                for cc in range(HC):
                    t.ldweights(wub[:, i % 2, cc, :])
                    t.matmul(P[:, pu0, :], wub[:, i % 2, cc, :],
                             xn[:, cc * 1024:cc * 1024 + 512],
                             start=(cc == 0), stop=(cc == HC - 1))
                    ins = t.matmul(P[:, pu1, :], wub[:, i % 2, cc, :],
                                   xn[:, cc * 1024 + 512:(cc + 1) * 1024],
                                   start=(cc == 0), stop=(cc == HC - 1))
                ins.then_inc(SM["s_mm3"], 1)
            t.wait_ge(SM["s_mm3"], IC)
            t.nop().then_inc(SM["b_p3"], 1)
            t.wait_ge(SM["b_p3"], 4)
            t.wait_ge(SM["s_adr"], 16)
            for o in range(8):
                if o >= 2:
                    t.wait_ge(SM["s_adb"], o - 1)
                ins = t.matmul(P[:, o % 2, :], onesf[0:1, :],
                               adrow[0:1, o * 512:(o + 1) * 512], start=True, stop=True)
                ins.then_inc(SM["s_adm"], 1)
            for oi in range(8):
                blk, o2 = oi // 4, oi % 4
                if o2 == 0:
                    t.wait_ge(SM["s_hld"], 16 * 4 * (blk + 1))
                    t.wait_ge(SM["s_atld"], 16 * (blk + 1))
                for m in range(2):
                    if oi == 0 and m == 0:
                        t.wait_ge(SM["s_adb"], 8)
                    if oi > 0:
                        t.wait_ge(SM["s_gb"], (oi - 1) * 8 + 4 * (m + 1))
                        t.wait_ge(SM["s_sgb"], (oi - 1) * 8 + 4 * (m + 1))
                    for icx in range(IC):
                        L = oi * NLD + icx // 4
                        slot = 8 * m + 4 * ((icx // 4) % 2) + icx % 4
                        t.wait_ge(SM[f"s_wd{2 * m + L % 2}"], 16 * (L // 2 + 1))
                        for sl in range(4):
                            t.ldweights(hblk[sl][:, icx, :])
                            ins = t.matmul(P[:, sl + 4 * m, :], hblk[sl][:, icx, :],
                                           wdb[:, slot, :],
                                           start=(icx == 0), stop=(icx == IC - 1))
                            if icx == IC - 1 and sl < 3:
                                ins.then_inc(SM["s_f4"], 1)
                        ins.then_inc(SM["s_wdc" if m == 0 else "s_wdc2"], 1)
                for jo in range(2):
                    t.wait_ge(SM["s_gb"], oi * 8 + 4 * (jo + 1))
                    for hcx in range(HC):
                        L = (2 * oi + jo) * 8 + hcx // 4
                        slot = 4 * (L % 2) + hcx % 4
                        t.wait_ge(SM[f"s_wo{L % 2}"], 16 * (L // 2 + 1))
                        for sl in range(4):
                            t.ldweights(atblk[:, hcx, sl * 128:(sl + 1) * 128])
                            ins = t.matmul(P[:, sl + 4 * jo, :],
                                           atblk[:, hcx, sl * 128:(sl + 1) * 128],
                                           wob[:, slot, :],
                                           start=(hcx == 0), stop=(hcx == HC - 1))
                            if hcx == HC - 1 and sl < 3:
                                ins.then_inc(SM["s_s4"], 1)
                        ins.then_inc(SM["s_woc"], 1)

        @block.vector
        def _(v):
            tv.mark(v.memset(ones_bf[:], 1.0))
            tv.mark(v.memset(onesf[:], 1.0))
            for i in range(NT):
                v.wait_ge(SM[f"s_x{i % 2}"], 16 * (i // 2 + 1))
                tv.wait_self(v)
                for j in range(4):
                    ins = v.bn_stats(bst[:, i % 2, j * 6:(j + 1) * 6],
                                     xtile[:, i % 2, j * 512:(j + 1) * 512])
                tv.mark(ins)
                v.wait_ge(SM[f"s_x{i % 2}b"], 16 * (i // 2 + 1))
                for j in range(4, 8):
                    ins = v.bn_stats(bst[:, i % 2, j * 6:(j + 1) * 6],
                                     xtile[:, i % 2, j * 512:(j + 1) * 512])
                tv.mark(ins)
                tv.wait_self(v)
                ins = v.bn_aggr(mv[:, i % 2, :], bst[:, i % 2, :])
                tv.mark(ins)
                tv.wait_self(v)
                ins = v.tensor_scalar_add(varp[:, i % 2, :], mv[:, i % 2, 1:2], EPS)
                tv.inc(ins, SM, "s_stat", 1)
                v.wait_ge(SM["s_rstd"], i + 1)
                if i >= 2:
                    v.wait_ge(SM["s_trp"], 8 * (i - 1))
                tv.wait_self(v)
                ins = v.tensor_scalar(xntok[:, i % 2, :], xtile[:, i % 2, :],
                                      mv[:, i % 2, 0:1], rstd[:, i % 2, 0:1],
                                      ALU.subtract, ALU.mult)
                tv.inc(ins, SM, "s_xn", 1)
            tv.wait_self(v)
            v.nop().then_inc(SM["b_p1"], 1)
            v.wait_ge(SM["b_p1"], 4)
            pr_cnt2 = 0
            rp_have = [0]
            pend_t = []
            pend_t2 = []

            def emit_tail_v(entry):
                th, tX, lpi = entry
                tg = gidx[(th, tX)]
                if tg >= 2:
                    v.wait_ge(SM[f"s_atsp{tX}"], 16 * (tg // 2))
                v.wait_ge(SM["s_ptfree"], lpi + 1)
                tv.wait_self(v)
                ins = v.tensor_copy(atstg[:, tX, :], P[:, 4 + 2 * (tg % 2), :])
                tv.inc(ins, SM, "s_atev", 1)
                if tg >= 2:
                    v.wait_ge(SM["s_rzmm"], tg - 1)
                tv.wait_self(v)
                ins = v.reciprocal(rzrow[0:1, :], P[0:1, 5, :])
                tv.inc(ins, SM, "s_zev", 1)
                tv.wait_self(v)
                ins = v.tensor_copy(rzbf[0:1, tX, :], rzrow[0:1, :])
                tv.inc(ins, SM, "s_rz", 1)
                pend_t2.append((th, tX, tg))

            def emit_tail_v2(entry):
                th, tX, tg = entry
                v.wait_ge(SM["s_rzmm"], tg + 1)
                tv.wait_self(v)
                ins = v.tensor_copy(rzbsb[:, tX, :], P[:, 4 + 2 * (tg % 2), :])
                tv.inc(ins, SM, "s_rzb", 1)
                th2 = tg // 2
                if th2 >= 1:
                    v.wait_ge(SM[f"s_atsp{tX}"], 16 * th2)
                tv.wait_self(v)
                ins = v.tensor_tensor(atn2[:, tX, :], atstg[:, tX, :],
                                      rzbsb[:, tX, :], ALU.mult)
                tv.inc(ins, SM, "s_atn2", 1)

            def emit_qtr(j, half):
                bk = 7 if j >= 2 else (0, 1, 4, 5)[2 * j + half]
                v.wait_ge(SM["s_qpmm"], 2 * j + half + 1)
                tv.wait_self(v)
                ins = v.tensor_scalar(qtb[:, j % 2, 512 * half:512 * half + 512],
                                      P[:, bk, :], SCALE, None, ALU.mult)
                tv.inc(ins, SM, "s_qtr", 1)

            for j in range(2):
                for half in range(2):
                    emit_qtr(j, half)

            for si, stp in enumerate(steps):
                _, h, X, kt, pr = stp
                li = si % NSH
                if li == 2 and 2 <= h + 1 < NH:
                    emit_qtr(h + 1, 1)
                if li == 16 and h + 2 < NH:
                    emit_qtr(h + 2, 0)
                lo, ns = CFG[X][kt]
                N = ns * 128
                first = (kt == 0 and pr == 0)
                last = (kt == NKT[X] - 1 and pr == 1)
                pb = si % 2
                ub = pr_cnt2 % 3
                o = RP_OFF[(X, kt)] + 2 * pr * N
                if si < NSH:
                    cneed = (o + 2 * N + 4607) // 4608
                    while rp_have[0] < cneed:
                        v.wait_ge(SM[f"s_wd{4 + rp_have[0]}"], 16)
                        rp_have[0] += 1
                v.wait_ge(SM["s_stepdone"], si + 1)
                if pr_cnt2 >= 3:
                    v.wait_ge(SM["s_pt"], pr_cnt2 - 2)
                tv.wait_self(v)
                ins = v.scalar_tensor_tensor(
                    usb[:, ub, :, 0:N],
                    rpsb[:, o:o + 2 * N].rearrange("p (a b) -> p a b", b=N),
                    float(SLOPES[h]), P[:, 2 * pb:2 * pb + 2, 0:N],
                    ALU.mult, ALU.add)
                tv.inc(ins, SM, "s_stepfree", 1)
                pr_cnt2 += 1
                if last:
                    pend_t.append((h, X, pair_idx[(h, X, kt, pr)]))
                while pend_t and pend_t[0][2] + 2 < pr_cnt2:
                    while pend_t2:
                        emit_tail_v2(pend_t2.pop(0))
                    emit_tail_v(pend_t.pop(0))
            while pend_t:
                while pend_t2:
                    emit_tail_v2(pend_t2.pop(0))
                emit_tail_v(pend_t.pop(0))
            while pend_t2:
                emit_tail_v2(pend_t2.pop(0))
            tv.wait_self(v)
            v.nop().then_inc(SM["b_p2"], 1)
            v.wait_ge(SM["b_p2"], 4)
            v.wait_ge(SM["s_agmm"], 1)
            tv.wait_self(v)
            ins = v.tensor_copy(ag_t[:], P[:, 0, 0:128])
            tv.inc(ins, SM, "s_agev", 1)
            v.wait_ge(SM["s_agmm"], 2)
            tv.wait_self(v)
            ins = v.tensor_copy(au_t[:], P[:, 1, 0:128])
            tv.inc(ins, SM, "s_agev", 1)
            for i in range(IC):
                v.wait_ge(SM["s_sil"], i + 1)
                b = i // 4
                bi = b % 2
                slt = i % 4
                if b >= 2:
                    v.wait_ge(SM[f"s_hsp{b % 2}"], 16 * 8 * (b // 2))
                pu0, pu1 = (2, 3) if i % 2 == 0 else (6, 7)
                tv.wait_self(v)
                ins = v.scalar_tensor_tensor(
                    hstg[:, bi, slt, 0:512], P[:, pu0, :], au_t[:, i:i + 1],
                    sgs[:, i % 2, 0, :], ALU.mult, ALU.mult)
                tv.mark(ins)
                ins = v.scalar_tensor_tensor(
                    hstg[:, bi, slt, 512:1024], P[:, pu1, :], au_t[:, i:i + 1],
                    sgs[:, i % 2, 1, :], ALU.mult, ALU.mult)
                tv.inc(ins, SM, "s_h3", 1)
            tv.wait_self(v)
            v.nop().then_inc(SM["b_p3"], 1)
            v.wait_ge(SM["b_p3"], 4)
            for oi in range(8):
                blk, o2 = oi // 4, oi % 4
                for j in range(2):
                    o = 2 * o2 + j
                    for sl in range(4):
                        k4 = (blk * 8 + o) * 4 + sl
                        if sl < 3:
                            v.wait_ge(SM["s_f4"], oi * 6 + 3 * j + sl + 1)
                        else:
                            v.wait_ge(SM["s_wdc" if j == 0 else "s_wdc2"],
                                      (oi + 1) * IC)
                        if k4 >= 2:
                            v.wait_ge(SM["s_g2"], k4 - 1)
                        tv.wait_self(v)
                        ins = v.tensor_tensor(gbuf[:, sl % 2, :], P[:, sl + 4 * j, :],
                                              adb[:, o * 512:(o + 1) * 512], ALU.mult)
                        tv.inc(ins, SM, "s_gb", 1)
                    for sl in range(4):
                        k4 = (blk * 8 + o) * 4 + sl
                        if sl < 3:
                            v.wait_ge(SM["s_s4"], oi * 6 + 3 * j + sl + 1)
                        else:
                            v.wait_ge(SM["s_woc"], (2 * oi + j + 1) * HC)
                        v.wait_ge(SM["s_g2"], k4 + 1)
                        tv.wait_self(v)
                        ins = v.scalar_tensor_tensor(sgb[:, sl % 2, :],
                                                     P[:, sl + 4 * j, :],
                                                     1.0, g2buf[:, sl % 2, :],
                                                     ALU.mult, ALU.mult)
                        tv.inc(ins, SM, "s_sgb", 1)
                        v.wait_ge(SM[f"s_xe{k4 % 4}"], 16 * (k4 // 4 + 1))
                        if k4 >= 2:
                            v.wait_ge(SM[f"s_out{k4 % 2}"], 16 * (k4 // 2))
                        tv.wait_self(v)
                        ins = v.tensor_tensor(oeb[:, sl % 2, :], sgb[:, sl % 2, :],
                                              xeb[:, k4 % 4, :], ALU.add)
                        tv.inc(ins, SM, "s_oe", 1)

        @block.scalar
        def _(s):
            trg = 0
            for i in range(NT):
                s.wait_ge(SM["s_stat"], i + 1)
                ta.wait_self(s)
                ins = s.activation(lcol[:, i % 2, :], varp[:, i % 2, :], AF.Ln)
                ta.mark(ins)
                ta.wait_self(s)
                ins = s.activation(rstd[:, i % 2, :], lcol[:, i % 2, :], AF.Exp,
                                   scale=-0.5)
                ta.inc(ins, SM, "s_rstd", 1)
                for gq in range(8):
                    s.wait_ge(SM["s_trp"], trg + 1)
                    bnk = trg % 2
                    bc = gq * 4
                    if i < 8:
                        dst = xnv[:, bc:bc + 4, i * 128:(i + 1) * 128]
                    else:
                        j = (i - 8) % 4
                        dst = xcv[:, bc:bc + 4, j * 128:(j + 1) * 128]
                    ta.wait_self(s)
                    ins = s.activation(dst,
                                       P[:, bnk, :].rearrange("p (c t) -> p c t", t=128),
                                       AF.Copy)
                    ta.inc(ins, SM, "s_trev", 1)
                    trg += 1
                if i >= 8 and (i - 8) % 4 == 3:
                    gi = (i - 8) // 4
                    s.wait_ge(SM["s_kv"], 2 * (gi + 1))
                    ta.wait_self(s)
                    ins = s.activation(ktsb[:, gi * 512:(gi + 1) * 512], P[:, 2, :],
                                       AF.Copy)
                    ta.inc(ins, SM, "s_kvev", 1)
                    if gi >= 2:
                        s.wait_ge(SM["s_vt"], gi - 1)
                    ta.wait_self(s)
                    ins = s.activation(vtstg[:, gi % 2, :], P[:, 3, :], AF.Copy)
                    ta.inc(ins, SM, "s_kvev", 1)
                    s.wait_ge(SM["s_vt"], gi + 1)
                    ta.wait_self(s)
                    ins = s.activation(vkm[:, gi * 512:(gi + 1) * 512], P[:, 4, :],
                                       AF.Copy)
                    ta.inc(ins, SM, "s_kvev", 1)
            ta.wait_self(s)
            s.nop().then_inc(SM["b_p1"], 1)
            s.wait_ge(SM["b_p1"], 4)
            pr_cnt3 = 0
            for si, stp in enumerate(steps):
                if stp[0] != "pr":
                    continue
                _, h, X, kt, pr = stp
                N = CFG[X][kt][1] * 128
                ubp = pr_cnt3 % 3
                ubu = pr_cnt3 % 3
                s.wait_ge(SM["s_stepfree"], si + 1)
                if pr_cnt3 >= 3:
                    s.wait_ge(SM["s_ptfree"], pr_cnt3 - 2)
                ta.wait_self(s)
                ins = s.activation(ptb[:, ubp, :, 0:N], usb[:, ubu, :, 0:N], AF.Exp)
                ta.inc(ins, SM, "s_pt", 1)
                pr_cnt3 += 1
            ta.wait_self(s)
            s.nop().then_inc(SM["b_p2"], 1)
            s.wait_ge(SM["b_p2"], 4)
            for i in range(IC):
                s.wait_ge(SM["s_mm3"], i + 1)
                pg0, pg1 = (0, 1) if i % 2 == 0 else (4, 5)
                if i >= 2:
                    s.wait_ge(SM["s_h3"], i - 1)
                ta.wait_self(s)
                ins = s.activation(sgs[:, i % 2, 0, :], P[:, pg0, :], AF.Silu,
                                   scale=ag_t[:, i:i + 1])
                ta.mark(ins)
                ins = s.activation(sgs[:, i % 2, 1, :], P[:, pg1, :], AF.Silu,
                                   scale=ag_t[:, i:i + 1])
                ta.inc(ins, SM, "s_sil", 1)
            ta.wait_self(s)
            s.nop().then_inc(SM["b_p3"], 1)
            s.wait_ge(SM["b_p3"], 4)
            s.wait_ge(SM["s_hsp0"], 16 * 8 * 11)
            s.wait_ge(SM["s_hsp1"], 16 * 8 * 11)
            for o in range(8):
                s.wait_ge(SM["s_adm"], o + 1)
                ta.wait_self(s)
                ins = s.activation(adb[:, o * 512:(o + 1) * 512], P[:, o % 2, :],
                                   AF.Copy)
                ta.inc(ins, SM, "s_adb", 1)
            for blk in range(2):
                for o in range(8):
                    for sl in range(4):
                        k4 = (blk * 8 + o) * 4 + sl
                        if k4 >= 4:
                            s.wait_ge(SM["s_oe"], k4 - 3)
                        s.dma_start(xeb[:, k4 % 4, :],
                                    x_own[(blk * 4 + sl) * 128:(blk * 4 + sl + 1) * 128,
                                          o * 512:(o + 1) * 512]
                                    ).then_inc(SM[f"s_xe{k4 % 4}"], 16)
                    for sl in range(4):
                        k4 = (blk * 8 + o) * 4 + sl
                        s.wait_ge(SM["s_gb"], k4 + 1)
                        if k4 >= 2:
                            s.wait_ge(SM["s_sgb"], k4 - 1)
                        ta.wait_self(s)
                        ins = s.activation(g2buf[:, sl % 2, :], gbuf[:, sl % 2, :],
                                           AF.Silu)
                        ta.inc(ins, SM, "s_g2", 1)
                    for sl in range(4):
                        k4 = (blk * 8 + o) * 4 + sl
                        s.wait_ge(SM["s_oe"], k4 + 1)
                        s.dma_start(out[(blk * 4 + sl) * 128:(blk * 4 + sl + 1) * 128,
                                        o * 512:(o + 1) * 512],
                                    oeb[:, sl % 2, :]).then_inc(SM[f"s_out{k4 % 2}"], 16)
            s.wait_ge(SM["s_out0"], 16 * 32)
            s.wait_ge(SM["s_out1"], 16 * 32)

    est.close()
    return nc


def _build_rp(p):
    rp = np.empty((128, RP_TOT), np.float32)
    for X in (0, 1):
        vs = [SLOT_V[s] for s in range(4 * X, 4 * X + 4)]
        for kt in range(NKT[X]):
            lo, ns = CFG[X][kt]
            N = ns * 128
            o = RP_OFF[(X, kt)]
            qg = np.concatenate([
                128 * (4 * (vs[lo + j] - 1) + p) + np.arange(128) for j in range(ns)])
            for ci in range(4):
                kg = kt * 512 + ci * 128 + np.arange(128)
                rel = (kg[:, None] - qg[None, :]).astype(np.float32)
                rp[:, o + ci * N:o + (ci + 1) * N] = np.where(rel > 0, NEG, rel)
    return rp


def _stage(inputs):
    x = np.asarray(inputs["L0_attn_hidden_states"])
    Wqkv = np.asarray(inputs["L0_attn_Wqkv"])
    Wo = np.asarray(inputs["L0_attn_Wo"])
    Wg = np.asarray(inputs["L0_ffn_Wg"])
    Wu = np.asarray(inputs["L0_ffn_Wu"])
    Wd = np.asarray(inputs["L0_ffn_Wd"])
    ag = np.asarray(inputs["L0_ffn_ag"])
    au = np.asarray(inputs["L0_ffn_au"])
    ad = np.asarray(inputs["L0_ffn_ad"])

    wq4 = np.ascontiguousarray(
        Wqkv[:, :HID].reshape(HC, 128, NH, 128).transpose(2, 1, 0, 3))
    wkv4 = np.ascontiguousarray(
        Wqkv[:, HID:].reshape(HC, 128, 2, 128).transpose(2, 1, 0, 3))
    wg4 = np.ascontiguousarray(Wg.reshape(IC, 128, HC, 128).transpose(0, 3, 2, 1))
    wu4 = np.ascontiguousarray(Wu.reshape(IC, 128, HC, 128).transpose(0, 3, 2, 1))
    wdt = np.ascontiguousarray(Wd.T)
    agd = np.ascontiguousarray(ag.reshape(IC, 128))
    aud = np.ascontiguousarray(au.reshape(IC, 128))
    ident = np.eye(128, dtype=np.float32)

    covs = [np.ascontiguousarray(
        np.concatenate([x[A, :COV_A], x[1 - A, :COV_B]], axis=0)) for A in (0, 1)]
    rp_cache = {}
    in_maps, meta = [], []
    for c in range(8):
        A, p = c // 4, c % 4
        slots = []
        for s_ in range(8):
            v = SLOT_V[s_]
            b = A if s_ < 4 else 1 - A
            slots.append((b, 4 * (v - 1) + p))
        x_own = np.ascontiguousarray(
            np.concatenate([x[b, 128 * t:128 * (t + 1)] for (b, t) in slots], axis=0))
        if p not in rp_cache:
            rp_cache[p] = _build_rp(p)
        in_maps.append({
            "x_own": x_own, "x_cov": covs[A], "rp": rp_cache[p],
            "wq4": wq4, "wkv4": wkv4, "wg4": wg4, "wu4": wu4, "wdt": wdt,
            "wo": Wo, "agd": agd, "aud": aud, "add_": ad, "ident": ident,
        })
        meta.append(slots)
    return in_maps, meta


def kernel(**inputs):
    in_maps, meta = _stage(inputs)
    nc = build_nc(dbg=False)
    res = run_bass_kernel_spmd(nc, in_maps, core_ids=list(range(8)))
    y = np.empty((B, S, HID), np.float32)
    for c in range(8):
        oc = res.results[c]["out"]
        for s_, (b, t) in enumerate(meta[c]):
            y[b, 128 * t:128 * (t + 1)] = oc[128 * s_:128 * (s_ + 1)]
    return y


# version 12 (final)
# speedup vs baseline: 1.1754x; 1.0244x over previous
import math
from contextlib import ExitStack

import numpy as np

import concourse.bass as bass
import concourse.mybir as mybir
from concourse.bass_utils import run_bass_kernel_spmd

F32 = mybir.dt.float32
BF16 = mybir.dt.bfloat16
AF = mybir.ActivationFunctionType
ALU = mybir.AluOpType

B, S, HID, NH, DH, INTER = 2, 4096, 4096, 32, 128, 11008
HC = HID // 128
IC = INTER // 128
SLOT_V = (2, 4, 6, 8, 1, 3, 5, 7)
COV_A, COV_B = 4096, 3584
NCOV = COV_A + COV_B
NKT = (8, 7)
SCALE = 1.0 / math.sqrt(DH)
EPS = 1e-5
NEG = -1e30
NT = 68
NG = 15


def _alibi_slopes(n=NH):
    cp2 = 2 ** math.floor(math.log2(n))
    base = 2.0 ** (-(2.0 ** (-(math.log2(cp2) - 3))))
    slopes = base ** np.arange(1, cp2 + 1, dtype=np.float64)
    if cp2 != n:
        extra_base = 2.0 ** (-(2.0 ** (-(math.log2(2 * cp2) - 3))))
        rem = min(cp2, n - cp2)
        extra = extra_base ** np.arange(1, 1 + 2 * rem, 2, dtype=np.float64)
        slopes = np.concatenate([slopes, extra])
    return slopes.astype(np.float64)


SLOPES = _alibi_slopes()


def _att_cfg():
    cfg = []
    for X in (0, 1):
        vs = [SLOT_V[s] for s in range(4 * X, 4 * X + 4)]
        cfg.append([([j for j, v in enumerate(vs) if v >= kt + 1][0],
                     len([j for j, v in enumerate(vs) if v >= kt + 1]))
                    for kt in range(NKT[X])])
    return cfg


CFG = _att_cfg()

RP_OFF = {}
_o = 0
for _X in (0, 1):
    for _kt in range(NKT[_X]):
        RP_OFF[(_X, _kt)] = _o
        _o += 4 * CFG[_X][_kt][1] * 128
RP_TOT = _o

M = {}
_c = 16512
def _put(name, nbytes):
    global _c
    M[name] = _c
    _c = (_c + nbytes + 31) // 32 * 32

_put("ones_bf", 256); _put("I_bf", 256); _put("idf", 512); _put("onesf", 512)
_put("ag_t", 512); _put("au_t", 512); _put("agnat", 512)
_put("bst", 384); _put("mv", 64); _put("varp", 32); _put("lcol", 32); _put("rstd", 32)
_put("rzbf", 2048); _put("rzrow", 2048)
assert _c <= 24704, _c
M["xn"] = 24704
M["xtile"] = 90240
M["xntok"] = 123008
M["xnTcov"] = 139392
M["wkvb"] = 172160
M["vtstg"] = 188544
M["ktsb"] = 198624
M["vkm"] = 213984
M["rpsb"] = 90240
M["wqb"] = 163968
M["qtb"] = 172160
M["usb"] = 174208
M["ptb"] = 186496
M["atstg"] = 192640
M["rzbsb"] = 194688
assert M["rzbsb"] + 2048 <= 198624
M["wgb"] = 90240
M["wub"] = 106656
M["sgs"] = 123072
M["hstg"] = 139488
assert 139488 + 16384 <= 229344
M["hblk"] = 24704
M["atblk"] = 112800
M["adb"] = 145600
M["adrow"] = 162016
M["wdb"] = 178432
M["wob"] = 194848
M["gbuf"] = 203072
M["g2buf"] = 207200
M["xeb"] = 211328
M["oeb"] = 219552
M["sgb"] = 223680
assert 223680 + 4096 <= 229344


class ETrack:

    def __init__(self, stack, nc, name, semc):
        self.h = stack.enter_context(nc.semaphore(name))
        self.n = 0
        self.last = None
        self.semc = semc

    def inc(self, ins, sm, name, k=1):
        ins.then_inc(sm[name], k)
        self.semc[name] = self.semc.get(name, 0) + k
        self.last = (sm[name], self.semc[name])
        return ins

    def mark(self, ins):
        self.n += 1
        ins.then_inc(self.h, 1)
        self.last = (self.h, self.n)
        return ins

    def wait_self(self, eng):
        if self.last is not None:
            eng.wait_ge(self.last[0], self.last[1])


def build_nc(dbg=False):
    nc = bass.Bass("TRN2", target_bir_lowering=False)

    x_own = nc.dram_tensor("x_own", [1024, HID], F32, kind="ExternalInput").ap()
    x_cov = nc.dram_tensor("x_cov", [NCOV, HID], F32, kind="ExternalInput").ap()
    wq4 = nc.dram_tensor("wq4", [NH, 128, HC, 128], F32, kind="ExternalInput").ap()
    wkv4 = nc.dram_tensor("wkv4", [2, 128, HC, 128], F32, kind="ExternalInput").ap()
    wg4 = nc.dram_tensor("wg4", [IC, 128, HC, 128], F32, kind="ExternalInput").ap()
    wu4 = nc.dram_tensor("wu4", [IC, 128, HC, 128], F32, kind="ExternalInput").ap()
    wdt = nc.dram_tensor("wdt", [INTER, HID], F32, kind="ExternalInput").ap()
    wo = nc.dram_tensor("wo", [HID, HID], F32, kind="ExternalInput").ap()
    agd = nc.dram_tensor("agd", [IC, 128], F32, kind="ExternalInput").ap()
    aud = nc.dram_tensor("aud", [IC, 128], F32, kind="ExternalInput").ap()
    add_ = nc.dram_tensor("add_", [HID], F32, kind="ExternalInput").ap()
    rp = nc.dram_tensor("rp", [128, RP_TOT], F32, kind="ExternalInput").ap()
    ident = nc.dram_tensor("ident", [128, 128], F32, kind="ExternalInput").ap()
    out = nc.dram_tensor("out", [1024, HID], F32, kind="ExternalOutput").ap()
    ikind = "ExternalOutput" if dbg else "Internal"
    wdt_bf = nc.dram_tensor("wdt_bf", [INTER, HID], BF16, kind="Internal").ap()
    wo_bf = nc.dram_tensor("wo_bf", [HID, HID], BF16, kind="Internal").ap()
    h_d = nc.dram_tensor("h_d", [8, 128, IC, 128], BF16, kind=ikind).ap()
    at_d = nc.dram_tensor("at_d", [2, 128, NH, 512], BF16, kind=ikind).ap()
    if dbg:
        xn_d = nc.dram_tensor("xn_d", [128, HC * 1024], BF16, kind="ExternalOutput").ap()
        kt_d = nc.dram_tensor("kt_d", [128, NCOV], BF16, kind="ExternalOutput").ap()
        vk_d = nc.dram_tensor("vk_d", [128, NCOV], BF16, kind="ExternalOutput").ap()

    def sb(name, shape, dt):
        return nc.alloc_sbuf_tensor_at(name, shape, dt, offset=M[name]).ap()

    ones_bf = sb("ones_bf", [128, 128], BF16)
    I_bf = sb("I_bf", [128, 128], BF16)
    idf = sb("idf", [128, 128], F32)
    onesf = sb("onesf", [128, 128], F32)
    ag_t = sb("ag_t", [128, 128], F32)
    au_t = sb("au_t", [128, 128], F32)
    agnat = sb("agnat", [128, 128], F32)
    bst = sb("bst", [128, 2, 48], F32)
    mv = sb("mv", [128, 2, 2], F32)
    varp = sb("varp", [128, 2, 1], F32)
    lcol = sb("lcol", [128, 2, 1], F32)
    rstd = sb("rstd", [128, 2, 1], F32)
    rzrow = sb("rzrow", [1, 512], F32)
    rzbf = sb("rzbf", [1, 2, 512], BF16)
    xn = sb("xn", [128, HC * 1024], BF16)
    xtile = sb("xtile", [128, 2, HID], F32)
    xntok = sb("xntok", [128, 2, HID], BF16)
    xnTcov = sb("xnTcov", [128, HC * 512], BF16)
    wkvb = sb("wkvb", [128, 2, HC, 128], BF16)
    vtstg = sb("vtstg", [128, 2, 512], BF16)
    ktsb = sb("ktsb", [128, NCOV], BF16)
    vkm = sb("vkm", [128, NCOV], BF16)
    rpsb = sb("rpsb", [128, RP_TOT], F32)
    wqb = sb("wqb", [128, HC, 128], BF16)
    qtb = sb("qtb", [128, 1024], BF16)
    usb = sb("usb", [128, 3, 2, 512], F32)
    ptb = sb("ptb", [128, 3, 2, 512], BF16)
    atstg = sb("atstg", [128, 2, 512], BF16)
    rzbsb = sb("rzbsb", [128, 2, 512], BF16)
    atn2 = atstg
    wgb = sb("wgb", [128, 2, HC, 128], BF16)
    wub = sb("wub", [128, 2, HC, 128], BF16)
    sgs = sb("sgs", [128, 2, 2, 512], F32)
    hstg = sb("hstg", [128, 2, 4, 1024], BF16)
    hblk = [nc.alloc_sbuf_tensor_at(f"hblk{i}", [128, IC, 128], BF16,
                                    offset=M["hblk"] + i * 22016).ap()
            for i in range(4)]
    atblk = sb("atblk", [128, NH, 512], BF16)
    adb = sb("adb", [128, HID], F32)
    adrow = sb("adrow", [1, HID], F32)
    wdb = sb("wdb", [128, 16, 512], BF16)
    wob = sb("wob", [128, 8, 512], BF16)
    gbuf = sb("gbuf", [128, 2, 512], F32)
    g2buf = sb("g2buf", [128, 2, 512], F32)
    xeb = sb("xeb", [128, 4, 512], F32)
    oeb = sb("oeb", [128, 2, 512], F32)
    sgb = sb("sgb", [128, 2, 512], F32)

    P = nc.alloc_psum_tensor("P", [128, 8, 512], F32).ap()
    xnv = xn.rearrange("p (c t) -> p c t", t=1024)
    xcv = xnTcov.rearrange("p (c t) -> p c t", t=512)

    est = ExitStack()
    SM = {}
    semnames = ["s_x0", "s_x1", "s_x0b", "s_x1b", "s_stat", "s_rstd", "s_xn", "s_trp", "s_trev",
                "s_kv", "s_kvev", "s_vt", "s_wkv", "s_ib",
                "b_p1", "s_rp", "s_wq", "s_stepdone", "s_stepfree",
                "s_pt", "s_ptfree", "s_zev", "s_rz", "s_rzmm",
                "s_rzb", "s_atev", "s_atn2", "s_atsp0", "s_atsp1",
                "b_p2", "s_agt", "s_agmm", "s_agev", "s_wg0", "s_wg1", "s_wu0",
                "s_wu1", "s_mm3", "s_sil", "s_h3", "s_hsp0", "s_hsp1",
                "b_p3", "s_hld", "s_atld", "s_adr", "s_adm", "s_adb",
                "s_wdc", "s_woc", "s_dbgd", "s_cvt",
                "s_f4", "s_s4", "s_gb", "s_g2", "s_sgb", "s_xe0", "s_xe1", "s_xe2", "s_xe3",
                "s_oe", "s_out0", "s_out1"]
    semnames += [f"s_wd{j}" for j in range(8)] + [f"s_wo{j}" for j in range(8)]
    for s_ in semnames:
        SM[s_] = est.enter_context(nc.semaphore(s_))

    SEMC = {}
    tv = ETrack(est, nc, "ss_v", SEMC)
    ta = ETrack(est, nc, "ss_a", SEMC)
    tp = ETrack(est, nc, "ss_p", SEMC)

    steps = []
    for h in range(NH):
        steps.append(("qp", h))
        for X in (0, 1):
            for kt in range(NKT[X]):
                for pr in (0, 1):
                    steps.append(("pr", h, X, kt, pr))
    NSTEP = len(steps)
    pair_idx = {}
    _pi = 0
    for stp in steps:
        if stp[0] == "pr":
            pair_idx[stp[1:]] = _pi
            _pi += 1
    gidx = {}
    _gi = 0
    for h in range(NH):
        for X in (0, 1):
            gidx[(h, X)] = _gi
            _gi += 1

    with nc.Block() as block:

        @block.sync
        def _(q):
            for i in range(NT):
                if i >= 2:
                    q.wait_ge(SM["s_xn"], i - 1)
                src = (x_own[i * 128:(i + 1) * 128, :] if i < 8
                       else x_cov[(i - 8) * 128:(i - 7) * 128, :])
                q.dma_start(xtile[:, i % 2, 0:2048], src[:, 0:2048]
                            ).then_inc(SM[f"s_x{i % 2}"], 16)
            q.wait_ge(SM["b_p1"], 4)
            q.dma_start(rpsb[:], rp[:, :]).then_inc(SM["s_rp"], 16)
            for h in range(NH):
                for X in (0, 1):
                    g = gidx[(h, X)]
                    q.wait_ge(SM["s_atn2"], g + 1)
                    q.dma_start(at_d[X, :, h, :], atn2[:, X, :]
                                ).then_inc(SM[f"s_atsp{X}"], 16)
            if dbg:
                q.wait_ge(SM["b_p2"], 4)
                q.dma_start(xn_d[:, :], xn[:]).then_inc(SM["s_dbgd"], 16)
                q.dma_start(kt_d[:, :], ktsb[:]).then_inc(SM["s_dbgd"], 16)
                q.dma_start(vk_d[:, :], vkm[:]).then_inc(SM["s_dbgd"], 16)
                q.wait_ge(SM["s_dbgd"], 48)
            nbatch = (IC + 3) // 4
            for b in range(nbatch):
                nch = min(4, IC - b * 4)
                q.wait_ge(SM["s_h3"], b * 4 + nch)
                for qt in range(8):
                    q.dma_start(h_d[qt, :, b * 4:b * 4 + nch, :],
                                hstg[:, b % 2, 0:nch, qt * 128:(qt + 1) * 128]
                                ).then_inc(SM[f"s_hsp{b % 2}"], 16)
            q.wait_ge(SM["b_p3"], 4)
            q.wait_ge(SM["s_hsp0"], 16 * 8 * 11)
            q.wait_ge(SM["s_hsp1"], 16 * 8 * 11)
            q.wait_ge(SM["s_atsp0"], 16 * NH)
            q.wait_ge(SM["s_atsp1"], 16 * NH)
            for blk in range(2):
                if blk == 1:
                    q.wait_ge(SM["s_wdc"], 8 * IC)
                for sl in range(4):
                    q.dma_start(hblk[sl][:, :, :], h_d[blk * 4 + sl, :, :, :]
                                ).then_inc(SM["s_hld"], 16)
                if blk == 1:
                    q.wait_ge(SM["s_woc"], 8 * HC)
                q.dma_start(atblk[:, :, :], at_d[blk, :, :, :]).then_inc(SM["s_atld"], 16)

        @block.gpsimd
        def _(g):
            g.dma_start(I_bf[:], ident[:, :]).then_inc(SM["s_ib"], 16)
            g.dma_start(idf[:], ident[:, :]).then_inc(SM["s_ib"], 16)
            conv = ([("d", i) for i in range(32)] + [("o", i) for i in range(16)])
            def emit_conv(entry):
                kind, i = entry
                if kind == "d":
                    g.dma_start(wdt_bf[i * 344:(i + 1) * 344, :],
                                wdt[i * 344:(i + 1) * 344, :]).then_inc(SM["s_cvt"], 16)
                else:
                    g.dma_start(wo_bf[i * 256:(i + 1) * 256, :],
                                wo[i * 256:(i + 1) * 256, :]).then_inc(SM["s_cvt"], 16)
            for i in range(NT):
                if i >= 2:
                    g.wait_ge(SM["s_xn"], i - 1)
                src = (x_own[i * 128:(i + 1) * 128, :] if i < 8
                       else x_cov[(i - 8) * 128:(i - 7) * 128, :])
                g.dma_start(xtile[:, i % 2, 2048:4096], src[:, 2048:4096]
                            ).then_inc(SM[f"s_x{i % 2}b"], 16)
                if i == 3:
                    for f in range(2):
                        g.dma_start(wkvb[:, f, :, :], wkv4[f, :, :, :]
                                    ).then_inc(SM["s_wkv"], 16)
            g.wait_ge(SM["s_xn"], NT)
            g.nop().then_inc(SM["b_p1"], 1)
            g.wait_ge(SM["b_p1"], 4)
            nsteph = 1 + 2 * (NKT[0] + NKT[1])
            for h in range(NH):
                if h >= 1:
                    g.wait_ge(SM["s_stepdone"], nsteph * (h - 1) + 1)
                g.dma_start(wqb[:, :, :], wq4[h, :, :, :]).then_inc(SM["s_wq"], 16)
                for _ in range(2):
                    if conv:
                        emit_conv(conv.pop(0))
            while conv:
                emit_conv(conv.pop(0))
            g.wait_ge(SM["s_wq"], 16 * NH)
            g.nop().then_inc(SM["b_p2"], 1)
            g.wait_ge(SM["b_p2"], 4)
            g.dma_start(agnat[0:86, :], agd[:, :]).then_inc(SM["s_agt"], 16)
            g.wait_ge(SM["s_agmm"], 1)
            g.dma_start(agnat[0:86, :], aud[:, :]).then_inc(SM["s_agt"], 16)
            for i in range(IC):
                if i >= 2:
                    g.wait_ge(SM["s_h3"], i - 1)
                g.dma_start(wgb[:, i % 2, :, :], wg4[i, :, :, :]
                            ).then_inc(SM[f"s_wg{i % 2}"], 16)
                g.dma_start(wub[:, i % 2, :, :], wu4[i, :, :, :]
                            ).then_inc(SM[f"s_wu{i % 2}"], 16)
            g.wait_ge(SM["s_wg0"], 16 * 43)
            g.wait_ge(SM["s_wg1"], 16 * 43)
            g.wait_ge(SM["s_wu0"], 16 * 43)
            g.wait_ge(SM["s_wu1"], 16 * 43)
            g.nop().then_inc(SM["b_p3"], 1)
            g.wait_ge(SM["b_p3"], 4)
            g.wait_ge(SM["s_atsp0"], 16 * NH)
            g.wait_ge(SM["s_atsp1"], 16 * NH)
            g.wait_ge(SM["s_hsp0"], 16 * 8 * 11)
            g.wait_ge(SM["s_hsp1"], 16 * 8 * 11)
            g.dma_start(adrow[0:1, :], add_[None, :]).then_inc(SM["s_adr"], 16)
            g.wait_ge(SM["s_cvt"], 16 * 48)
            wdL = 0
            woL = 0
            for oi in range(8):
                blk, o2 = oi // 4, oi % 4
                for j in range((IC + 3) // 4):
                    c = min(4, IC - 4 * j)
                    t0 = 2 * IC * oi + 8 * j
                    if oi > 0 and j < 2:
                        g.wait_ge(SM["s_wdc"], 2 * IC * oi)
                    elif j >= 2:
                        g.wait_ge(SM["s_wdc"], t0 - 8)
                    src = wdt_bf[4 * j * 128:(4 * j + c) * 128,
                                 o2 * 1024:(o2 + 1) * 1024].rearrange(
                                     "(a p) (m n) -> p a m n", p=128, n=512)
                    g.dma_start(wdb[:, 8 * (j % 2):8 * (j % 2) + 2 * c, :]
                                .rearrange("p (a m) n -> p a m n", m=2), src
                                ).then_inc(SM[f"s_wd{wdL % 2}"], 16)
                    wdL += 1
                for jo in range(2):
                    o = 2 * o2 + jo
                    for j in range(HC // 4):
                        t0 = HC * (2 * oi + jo) + 4 * j
                        if t0 >= 8:
                            g.wait_ge(SM["s_woc"], t0 - 4)
                        src = wo_bf[4 * j * 128:(4 * j + 4) * 128,
                                    o * 512:(o + 1) * 512].rearrange(
                                        "(a p) n -> p a n", p=128)
                        g.dma_start(wob[:, 4 * (woL % 2):4 * (woL % 2) + 4, :], src
                                    ).then_inc(SM[f"s_wo{woL % 2}"], 16)
                        woL += 1

        @block.tensor
        def _(t):
            t.wait_ge(SM["s_ib"], 32)
            trg = 0
            for i in range(NT):
                t.wait_ge(SM["s_xn"], i + 1)
                for cc in range(HC):
                    if trg >= 2:
                        t.wait_ge(SM["s_trev"], trg - 1)
                    qd = cc % 4
                    ins = t.matmul(P[:, trg % 2, qd * 128:(qd + 1) * 128],
                                   xntok[:, i % 2, cc * 128:(cc + 1) * 128],
                                   I_bf[:], start=True, stop=True)
                    if qd == 3:
                        ins.then_inc(SM["s_trp"], 1)
                        trg += 1
                if i >= 8 and (i - 8) % 4 == 3:
                    gi = (i - 8) // 4
                    t.wait_ge(SM["s_trev"], trg)
                    t.wait_ge(SM["s_wkv"], 32)
                    if gi >= 1:
                        t.wait_ge(SM["s_kvev"], 3 * gi)
                    for f in range(2):
                        for cc in range(HC):
                            t.ldweights(wkvb[:, f, cc, :])
                            ins = t.matmul(P[:, 2 + f, :], wkvb[:, f, cc, :],
                                           xnTcov[:, cc * 512:(cc + 1) * 512],
                                           start=(cc == 0), stop=(cc == HC - 1))
                        ins.then_inc(SM["s_kv"], 1)
                    t.wait_ge(SM["s_kvev"], 3 * gi + 2)
                    for kc in range(4):
                        ins = t.matmul(P[:, 4, kc * 128:(kc + 1) * 128],
                                       vtstg[:, gi % 2, kc * 128:(kc + 1) * 128],
                                       I_bf[:], start=True, stop=True)
                    ins.then_inc(SM["s_vt"], 1)
                    if gi == NG - 1:
                        t.wait_ge(SM["s_vt"], NG)
                        t.nop().then_inc(SM["b_p1"], 1)
            t.wait_ge(SM["b_p1"], 4)
            pr_cnt = 0
            tail = None
            tail_armed = -10
            cur_si = [0]
            pend = []
            ATT_LAG = 3

            def emit_attnv(d):
                nonlocal tail
                h, X, kt, pr, n, first, last, N, lo, kcb = d
                g = gidx[(h, X)]
                t.wait_ge(SM["s_pt"], n + 1)
                ab = 4 + 2 * (g % 2)
                if first:
                    if g >= 2:
                        t.wait_ge(SM["s_atev"], g - 1)
                    if g >= 1:
                        t.wait_ge(SM["s_zev"], g)
                ub = n % 3
                co = lo * 128
                for j in (0, 1):
                    ci = 2 * pr + j
                    t.ldweights(vkm[:, (kcb + ci) * 128:(kcb + ci + 1) * 128])
                    t.matmul(P[:, ab, co:co + N],
                             vkm[:, (kcb + ci) * 128:(kcb + ci + 1) * 128],
                             ptb[:, ub, j, 0:N],
                             start=(first and j == 0), stop=(last and j == 1))
                t.ldweights(ones_bf[:, 0:1])
                for j in (0, 1):
                    ins = t.matmul(P[0:1, 5, co:co + N], ones_bf[:, 0:1],
                                   ptb[:, ub, j, 0:N],
                                   start=(first and j == 0), stop=(last and j == 1))
                ins.then_inc(SM["s_ptfree"], 1)
                if last:
                    tail = (h, X)
                    nonlocal tail_armed
                    tail_armed = cur_si[0]

            def emit_tail_mm():
                nonlocal tail
                th, tX = tail
                tg = gidx[(th, tX)]
                t.wait_ge(SM["s_rz"], tg + 1)
                if tg >= 1:
                    t.wait_ge(SM["s_rzb"], tg)
                ins = t.matmul(P[:, 7, :], ones_bf[0:1, :], rzbf[0:1, tX, :],
                               start=True, stop=True)
                ins.then_inc(SM["s_rzmm"], 1)
                tail = None
                return ins

            for si, stp in enumerate(steps):
                cur_si[0] = si
                if si >= 2:
                    t.wait_ge(SM["s_stepfree"], si - 1)
                pb = si % 2
                if stp[0] == "qp":
                    h = stp[1]
                    t.wait_ge(SM["s_wq"], 16 * (h + 1))
                    for cc in range(HC):
                        t.ldweights(wqb[:, cc, :])
                        t.matmul(P[:, 2 * pb, :], wqb[:, cc, :],
                                 xn[:, cc * 1024:cc * 1024 + 512],
                                 start=(cc == 0), stop=(cc == HC - 1))
                        ins = t.matmul(P[:, 2 * pb + 1, :], wqb[:, cc, :],
                                       xn[:, cc * 1024 + 512:(cc + 1) * 1024],
                                       start=(cc == 0), stop=(cc == HC - 1))
                    ins.then_inc(SM["s_stepdone"], 1)
                    if tail is not None and si > tail_armed + 8:
                        emit_tail_mm()
                else:
                    _, h, X, kt, pr = stp
                    lo, ns = CFG[X][kt]
                    N = ns * 128
                    qoff = X * 512 + lo * 128
                    kcol = X * COV_A + kt * 512
                    kcb = X * 32 + kt * 4
                    first = (kt == 0 and pr == 0)
                    last = (kt == NKT[X] - 1 and pr == 1)
                    g = gidx[(h, X)]
                    if first and X == 0:
                        t.wait_ge(SM["s_stepfree"], si)
                    for j in (0, 1):
                        ci = 2 * pr + j
                        t.ldweights(ktsb[:, kcol + ci * 128: kcol + (ci + 1) * 128])
                        ins = t.matmul(P[:, 2 * pb + j, 0:N],
                                       ktsb[:, kcol + ci * 128: kcol + (ci + 1) * 128],
                                       qtb[:, qoff:qoff + N],
                                       start=True, stop=True)
                    ins.then_inc(SM["s_stepdone"], 1)
                    if tail is not None and si > tail_armed + 8:
                        emit_tail_mm()
                    pend.append((h, X, kt, pr, pr_cnt, first, last, N, lo, kcb))
                    pr_cnt += 1
                    while len(pend) > ATT_LAG:
                        emit_attnv(pend.pop(0))
            while pend:
                emit_attnv(pend.pop(0))
            emit_tail_mm()
            t.wait_ge(SM["s_rzmm"], 2 * NH)
            t.nop().then_inc(SM["b_p2"], 1)
            t.wait_ge(SM["b_p2"], 4)
            t.wait_ge(SM["s_agt"], 16)
            t.matmul(P[:, 0, 0:128], agnat[0:86, :], idf[0:86, :], start=True,
                     stop=True).then_inc(SM["s_agmm"], 1)
            t.wait_ge(SM["s_agt"], 32)
            t.matmul(P[:, 1, 0:128], agnat[0:86, :], idf[0:86, :], start=True,
                     stop=True).then_inc(SM["s_agmm"], 1)
            t.wait_ge(SM["s_agev"], 2)
            for i in range(IC):
                t.wait_ge(SM[f"s_wg{i % 2}"], 16 * (i // 2 + 1))
                t.wait_ge(SM[f"s_wu{i % 2}"], 16 * (i // 2 + 1))
                if i >= 2:
                    t.wait_ge(SM["s_h3"], i - 1)
                pg0, pg1 = (0, 1) if i % 2 == 0 else (4, 5)
                pu0, pu1 = (2, 3) if i % 2 == 0 else (6, 7)
                for cc in range(HC):
                    t.ldweights(wgb[:, i % 2, cc, :])
                    t.matmul(P[:, pg0, :], wgb[:, i % 2, cc, :],
                             xn[:, cc * 1024:cc * 1024 + 512],
                             start=(cc == 0), stop=(cc == HC - 1))
                    t.matmul(P[:, pg1, :], wgb[:, i % 2, cc, :],
                             xn[:, cc * 1024 + 512:(cc + 1) * 1024],
                             start=(cc == 0), stop=(cc == HC - 1))
                for cc in range(HC):
                    t.ldweights(wub[:, i % 2, cc, :])
                    t.matmul(P[:, pu0, :], wub[:, i % 2, cc, :],
                             xn[:, cc * 1024:cc * 1024 + 512],
                             start=(cc == 0), stop=(cc == HC - 1))
                    ins = t.matmul(P[:, pu1, :], wub[:, i % 2, cc, :],
                                   xn[:, cc * 1024 + 512:(cc + 1) * 1024],
                                   start=(cc == 0), stop=(cc == HC - 1))
                ins.then_inc(SM["s_mm3"], 1)
            t.wait_ge(SM["s_mm3"], IC)
            t.nop().then_inc(SM["b_p3"], 1)
            t.wait_ge(SM["b_p3"], 4)
            t.wait_ge(SM["s_adr"], 16)
            for o in range(8):
                if o >= 2:
                    t.wait_ge(SM["s_adb"], o - 1)
                ins = t.matmul(P[:, o % 2, :], onesf[0:1, :],
                               adrow[0:1, o * 512:(o + 1) * 512], start=True, stop=True)
                ins.then_inc(SM["s_adm"], 1)
            for oi in range(8):
                blk, o2 = oi // 4, oi % 4
                if o2 == 0:
                    t.wait_ge(SM["s_hld"], 16 * 4 * (blk + 1))
                    t.wait_ge(SM["s_atld"], 16 * (blk + 1))
                if oi == 0:
                    t.wait_ge(SM["s_adb"], 8)
                else:
                    t.wait_ge(SM["s_gb"], oi * 8)
                    t.wait_ge(SM["s_sgb"], oi * 8)
                for icx in range(IC):
                    L = oi * 22 + icx // 4
                    slot = 8 * ((icx // 4) % 2) + 2 * (icx % 4)
                    t.wait_ge(SM[f"s_wd{L % 2}"], 16 * (L // 2 + 1))
                    for sl in range(4):
                        t.ldweights(hblk[sl][:, icx, :])
                        for j in (0, 1):
                            ins = t.matmul(P[:, sl + 4 * j, :], hblk[sl][:, icx, :],
                                           wdb[:, slot + j, :],
                                           start=(icx == 0), stop=(icx == IC - 1))
                            if icx == IC - 1 and not (sl == 3 and j == 1):
                                ins.then_inc(SM["s_f4"], 1)
                    ins.then_inc(SM["s_wdc"], 2)
                for jo in range(2):
                    t.wait_ge(SM["s_gb"], oi * 8 + 4 * (jo + 1))
                    for hcx in range(HC):
                        L = (2 * oi + jo) * 8 + hcx // 4
                        slot = 4 * (L % 2) + hcx % 4
                        t.wait_ge(SM[f"s_wo{L % 2}"], 16 * (L // 2 + 1))
                        for sl in range(4):
                            t.ldweights(atblk[:, hcx, sl * 128:(sl + 1) * 128])
                            ins = t.matmul(P[:, sl + 4 * jo, :],
                                           atblk[:, hcx, sl * 128:(sl + 1) * 128],
                                           wob[:, slot, :],
                                           start=(hcx == 0), stop=(hcx == HC - 1))
                            if hcx == HC - 1 and sl < 3:
                                ins.then_inc(SM["s_s4"], 1)
                        ins.then_inc(SM["s_woc"], 1)

        @block.vector
        def _(v):
            tv.mark(v.memset(ones_bf[:], 1.0))
            tv.mark(v.memset(onesf[:], 1.0))
            for i in range(NT):
                v.wait_ge(SM[f"s_x{i % 2}"], 16 * (i // 2 + 1))
                tv.wait_self(v)
                for j in range(4):
                    ins = v.bn_stats(bst[:, i % 2, j * 6:(j + 1) * 6],
                                     xtile[:, i % 2, j * 512:(j + 1) * 512])
                tv.mark(ins)
                v.wait_ge(SM[f"s_x{i % 2}b"], 16 * (i // 2 + 1))
                for j in range(4, 8):
                    ins = v.bn_stats(bst[:, i % 2, j * 6:(j + 1) * 6],
                                     xtile[:, i % 2, j * 512:(j + 1) * 512])
                tv.mark(ins)
                tv.wait_self(v)
                ins = v.bn_aggr(mv[:, i % 2, :], bst[:, i % 2, :])
                tv.mark(ins)
                tv.wait_self(v)
                ins = v.tensor_scalar_add(varp[:, i % 2, :], mv[:, i % 2, 1:2], EPS)
                tv.inc(ins, SM, "s_stat", 1)
                v.wait_ge(SM["s_rstd"], i + 1)
                if i >= 2:
                    v.wait_ge(SM["s_trp"], 8 * (i - 1))
                tv.wait_self(v)
                ins = v.tensor_scalar(xntok[:, i % 2, :], xtile[:, i % 2, :],
                                      mv[:, i % 2, 0:1], rstd[:, i % 2, 0:1],
                                      ALU.subtract, ALU.mult)
                tv.inc(ins, SM, "s_xn", 1)
            tv.wait_self(v)
            v.nop().then_inc(SM["b_p1"], 1)
            v.wait_ge(SM["b_p1"], 4)
            v.wait_ge(SM["s_rp"], 16)
            pr_cnt2 = 0
            pend_t = []
            pend_t2 = []

            def emit_tail_v(entry):
                th, tX, lpi = entry
                tg = gidx[(th, tX)]
                if tg >= 2:
                    v.wait_ge(SM[f"s_atsp{tX}"], 16 * (tg // 2))
                v.wait_ge(SM["s_ptfree"], lpi + 1)
                tv.wait_self(v)
                ins = v.tensor_copy(atstg[:, tX, :], P[:, 4 + 2 * (tg % 2), :])
                tv.inc(ins, SM, "s_atev", 1)
                if tg >= 2:
                    v.wait_ge(SM["s_rzmm"], tg - 1)
                tv.wait_self(v)
                ins = v.reciprocal(rzrow[0:1, :], P[0:1, 5, :])
                tv.inc(ins, SM, "s_zev", 1)
                tv.wait_self(v)
                ins = v.tensor_copy(rzbf[0:1, tX, :], rzrow[0:1, :])
                tv.inc(ins, SM, "s_rz", 1)
                pend_t2.append((th, tX, tg))

            def emit_tail_v2(entry):
                th, tX, tg = entry
                v.wait_ge(SM["s_rzmm"], tg + 1)
                tv.wait_self(v)
                ins = v.tensor_copy(rzbsb[:, tX, :], P[:, 7, :])
                tv.inc(ins, SM, "s_rzb", 1)
                th2 = tg // 2
                if th2 >= 1:
                    v.wait_ge(SM[f"s_atsp{tX}"], 16 * th2)
                tv.wait_self(v)
                ins = v.tensor_tensor(atn2[:, tX, :], atstg[:, tX, :],
                                      rzbsb[:, tX, :], ALU.mult)
                tv.inc(ins, SM, "s_atn2", 1)

            for si, stp in enumerate(steps):
                if stp[0] == "qp":
                    h = stp[1]
                    v.wait_ge(SM["s_stepdone"], si + 1)
                    tv.wait_self(v)
                    ins = v.tensor_scalar(
                        qtb[:, :].rearrange("p (a b) -> p a b", a=2),
                        P[:, 2 * (si % 2):2 * (si % 2) + 2, :],
                        SCALE, None, ALU.mult)
                    tv.inc(ins, SM, "s_stepfree", 1)
                else:
                    _, h, X, kt, pr = stp
                    lo, ns = CFG[X][kt]
                    N = ns * 128
                    first = (kt == 0 and pr == 0)
                    last = (kt == NKT[X] - 1 and pr == 1)
                    pb = si % 2
                    ub = pr_cnt2 % 3
                    o = RP_OFF[(X, kt)] + 2 * pr * N
                    v.wait_ge(SM["s_stepdone"], si + 1)
                    if pr_cnt2 >= 3:
                        v.wait_ge(SM["s_pt"], pr_cnt2 - 2)
                    tv.wait_self(v)
                    ins = v.scalar_tensor_tensor(
                        usb[:, ub, :, 0:N],
                        rpsb[:, o:o + 2 * N].rearrange("p (a b) -> p a b", b=N),
                        float(SLOPES[h]), P[:, 2 * pb:2 * pb + 2, 0:N],
                        ALU.mult, ALU.add)
                    tv.inc(ins, SM, "s_stepfree", 1)
                    pr_cnt2 += 1
                    if last:
                        pend_t.append((h, X, pair_idx[(h, X, kt, pr)]))
                    while pend_t and pend_t[0][2] + 2 < pr_cnt2:
                        while pend_t2:
                            emit_tail_v2(pend_t2.pop(0))
                        emit_tail_v(pend_t.pop(0))
            while pend_t:
                while pend_t2:
                    emit_tail_v2(pend_t2.pop(0))
                emit_tail_v(pend_t.pop(0))
            while pend_t2:
                emit_tail_v2(pend_t2.pop(0))
            tv.wait_self(v)
            v.nop().then_inc(SM["b_p2"], 1)
            v.wait_ge(SM["b_p2"], 4)
            v.wait_ge(SM["s_agmm"], 1)
            tv.wait_self(v)
            ins = v.tensor_copy(ag_t[:], P[:, 0, 0:128])
            tv.inc(ins, SM, "s_agev", 1)
            v.wait_ge(SM["s_agmm"], 2)
            tv.wait_self(v)
            ins = v.tensor_copy(au_t[:], P[:, 1, 0:128])
            tv.inc(ins, SM, "s_agev", 1)
            for i in range(IC):
                v.wait_ge(SM["s_sil"], i + 1)
                b = i // 4
                bi = b % 2
                slt = i % 4
                if b >= 2:
                    v.wait_ge(SM[f"s_hsp{b % 2}"], 16 * 8 * (b // 2))
                pu0, pu1 = (2, 3) if i % 2 == 0 else (6, 7)
                tv.wait_self(v)
                ins = v.scalar_tensor_tensor(
                    hstg[:, bi, slt, 0:512], P[:, pu0, :], au_t[:, i:i + 1],
                    sgs[:, i % 2, 0, :], ALU.mult, ALU.mult)
                tv.mark(ins)
                ins = v.scalar_tensor_tensor(
                    hstg[:, bi, slt, 512:1024], P[:, pu1, :], au_t[:, i:i + 1],
                    sgs[:, i % 2, 1, :], ALU.mult, ALU.mult)
                tv.inc(ins, SM, "s_h3", 1)
            tv.wait_self(v)
            v.nop().then_inc(SM["b_p3"], 1)
            v.wait_ge(SM["b_p3"], 4)
            for oi in range(8):
                blk, o2 = oi // 4, oi % 4
                for j in range(2):
                    o = 2 * o2 + j
                    for sl in range(4):
                        k4 = (blk * 8 + o) * 4 + sl
                        idx = 2 * sl + j
                        if idx < 7:
                            v.wait_ge(SM["s_f4"], oi * 7 + idx + 1)
                        else:
                            v.wait_ge(SM["s_wdc"], (oi + 1) * 2 * IC)
                        if k4 >= 2:
                            v.wait_ge(SM["s_g2"], k4 - 1)
                        tv.wait_self(v)
                        ins = v.tensor_tensor(gbuf[:, sl % 2, :], P[:, sl + 4 * j, :],
                                              adb[:, o * 512:(o + 1) * 512], ALU.mult)
                        tv.inc(ins, SM, "s_gb", 1)
                    for sl in range(4):
                        k4 = (blk * 8 + o) * 4 + sl
                        if sl < 3:
                            v.wait_ge(SM["s_s4"], oi * 6 + 3 * j + sl + 1)
                        else:
                            v.wait_ge(SM["s_woc"], (2 * oi + j + 1) * HC)
                        v.wait_ge(SM["s_g2"], k4 + 1)
                        tv.wait_self(v)
                        ins = v.scalar_tensor_tensor(sgb[:, sl % 2, :],
                                                     P[:, sl + 4 * j, :],
                                                     1.0, g2buf[:, sl % 2, :],
                                                     ALU.mult, ALU.mult)
                        tv.inc(ins, SM, "s_sgb", 1)
                        v.wait_ge(SM[f"s_xe{k4 % 4}"], 16 * (k4 // 4 + 1))
                        if k4 >= 2:
                            v.wait_ge(SM[f"s_out{k4 % 2}"], 16 * (k4 // 2))
                        tv.wait_self(v)
                        ins = v.tensor_tensor(oeb[:, sl % 2, :], sgb[:, sl % 2, :],
                                              xeb[:, k4 % 4, :], ALU.add)
                        tv.inc(ins, SM, "s_oe", 1)

        @block.scalar
        def _(s):
            trg = 0
            for i in range(NT):
                s.wait_ge(SM["s_stat"], i + 1)
                ta.wait_self(s)
                ins = s.activation(lcol[:, i % 2, :], varp[:, i % 2, :], AF.Ln)
                ta.mark(ins)
                ta.wait_self(s)
                ins = s.activation(rstd[:, i % 2, :], lcol[:, i % 2, :], AF.Exp,
                                   scale=-0.5)
                ta.inc(ins, SM, "s_rstd", 1)
                for gq in range(8):
                    s.wait_ge(SM["s_trp"], trg + 1)
                    bnk = trg % 2
                    bc = gq * 4
                    if i < 8:
                        dst = xnv[:, bc:bc + 4, i * 128:(i + 1) * 128]
                    else:
                        j = (i - 8) % 4
                        dst = xcv[:, bc:bc + 4, j * 128:(j + 1) * 128]
                    ta.wait_self(s)
                    ins = s.activation(dst,
                                       P[:, bnk, :].rearrange("p (c t) -> p c t", t=128),
                                       AF.Copy)
                    ta.inc(ins, SM, "s_trev", 1)
                    trg += 1
                if i >= 8 and (i - 8) % 4 == 3:
                    gi = (i - 8) // 4
                    s.wait_ge(SM["s_kv"], 2 * (gi + 1))
                    ta.wait_self(s)
                    ins = s.activation(ktsb[:, gi * 512:(gi + 1) * 512], P[:, 2, :],
                                       AF.Copy)
                    ta.inc(ins, SM, "s_kvev", 1)
                    if gi >= 2:
                        s.wait_ge(SM["s_vt"], gi - 1)
                    ta.wait_self(s)
                    ins = s.activation(vtstg[:, gi % 2, :], P[:, 3, :], AF.Copy)
                    ta.inc(ins, SM, "s_kvev", 1)
                    s.wait_ge(SM["s_vt"], gi + 1)
                    ta.wait_self(s)
                    ins = s.activation(vkm[:, gi * 512:(gi + 1) * 512], P[:, 4, :],
                                       AF.Copy)
                    ta.inc(ins, SM, "s_kvev", 1)
            ta.wait_self(s)
            s.nop().then_inc(SM["b_p1"], 1)
            s.wait_ge(SM["b_p1"], 4)
            pr_cnt3 = 0
            for si, stp in enumerate(steps):
                if stp[0] != "pr":
                    continue
                _, h, X, kt, pr = stp
                N = CFG[X][kt][1] * 128
                ubp = pr_cnt3 % 3
                ubu = pr_cnt3 % 3
                s.wait_ge(SM["s_stepfree"], si + 1)
                if pr_cnt3 >= 3:
                    s.wait_ge(SM["s_ptfree"], pr_cnt3 - 2)
                ta.wait_self(s)
                ins = s.activation(ptb[:, ubp, :, 0:N], usb[:, ubu, :, 0:N], AF.Exp)
                ta.inc(ins, SM, "s_pt", 1)
                pr_cnt3 += 1
            ta.wait_self(s)
            s.nop().then_inc(SM["b_p2"], 1)
            s.wait_ge(SM["b_p2"], 4)
            for i in range(IC):
                s.wait_ge(SM["s_mm3"], i + 1)
                pg0, pg1 = (0, 1) if i % 2 == 0 else (4, 5)
                if i >= 2:
                    s.wait_ge(SM["s_h3"], i - 1)
                ta.wait_self(s)
                ins = s.activation(sgs[:, i % 2, 0, :], P[:, pg0, :], AF.Silu,
                                   scale=ag_t[:, i:i + 1])
                ta.mark(ins)
                ins = s.activation(sgs[:, i % 2, 1, :], P[:, pg1, :], AF.Silu,
                                   scale=ag_t[:, i:i + 1])
                ta.inc(ins, SM, "s_sil", 1)
            ta.wait_self(s)
            s.nop().then_inc(SM["b_p3"], 1)
            s.wait_ge(SM["b_p3"], 4)
            s.wait_ge(SM["s_hsp0"], 16 * 8 * 11)
            s.wait_ge(SM["s_hsp1"], 16 * 8 * 11)
            for o in range(8):
                s.wait_ge(SM["s_adm"], o + 1)
                ta.wait_self(s)
                ins = s.activation(adb[:, o * 512:(o + 1) * 512], P[:, o % 2, :],
                                   AF.Copy)
                ta.inc(ins, SM, "s_adb", 1)
            for blk in range(2):
                for o in range(8):
                    for sl in range(4):
                        k4 = (blk * 8 + o) * 4 + sl
                        if k4 >= 4:
                            s.wait_ge(SM["s_oe"], k4 - 3)
                        s.dma_start(xeb[:, k4 % 4, :],
                                    x_own[(blk * 4 + sl) * 128:(blk * 4 + sl + 1) * 128,
                                          o * 512:(o + 1) * 512]
                                    ).then_inc(SM[f"s_xe{k4 % 4}"], 16)
                    for sl in range(4):
                        k4 = (blk * 8 + o) * 4 + sl
                        s.wait_ge(SM["s_gb"], k4 + 1)
                        if k4 >= 2:
                            s.wait_ge(SM["s_sgb"], k4 - 1)
                        ta.wait_self(s)
                        ins = s.activation(g2buf[:, sl % 2, :], gbuf[:, sl % 2, :],
                                           AF.Silu)
                        ta.inc(ins, SM, "s_g2", 1)
                    for sl in range(4):
                        k4 = (blk * 8 + o) * 4 + sl
                        s.wait_ge(SM["s_oe"], k4 + 1)
                        s.dma_start(out[(blk * 4 + sl) * 128:(blk * 4 + sl + 1) * 128,
                                        o * 512:(o + 1) * 512],
                                    oeb[:, sl % 2, :]).then_inc(SM[f"s_out{k4 % 2}"], 16)
            s.wait_ge(SM["s_out0"], 16 * 32)
            s.wait_ge(SM["s_out1"], 16 * 32)

    est.close()
    return nc


def _build_rp(p):
    rp = np.empty((128, RP_TOT), np.float32)
    for X in (0, 1):
        vs = [SLOT_V[s] for s in range(4 * X, 4 * X + 4)]
        for kt in range(NKT[X]):
            lo, ns = CFG[X][kt]
            N = ns * 128
            o = RP_OFF[(X, kt)]
            qg = np.concatenate([
                128 * (4 * (vs[lo + j] - 1) + p) + np.arange(128) for j in range(ns)])
            for ci in range(4):
                kg = kt * 512 + ci * 128 + np.arange(128)
                rel = (kg[:, None] - qg[None, :]).astype(np.float32)
                rp[:, o + ci * N:o + (ci + 1) * N] = np.where(rel > 0, NEG, rel)
    return rp


def _stage(inputs):
    x = np.asarray(inputs["L0_attn_hidden_states"])
    Wqkv = np.asarray(inputs["L0_attn_Wqkv"])
    Wo = np.asarray(inputs["L0_attn_Wo"])
    Wg = np.asarray(inputs["L0_ffn_Wg"])
    Wu = np.asarray(inputs["L0_ffn_Wu"])
    Wd = np.asarray(inputs["L0_ffn_Wd"])
    ag = np.asarray(inputs["L0_ffn_ag"])
    au = np.asarray(inputs["L0_ffn_au"])
    ad = np.asarray(inputs["L0_ffn_ad"])

    wq4 = np.ascontiguousarray(
        Wqkv[:, :HID].reshape(HC, 128, NH, 128).transpose(2, 1, 0, 3))
    wkv4 = np.ascontiguousarray(
        Wqkv[:, HID:].reshape(HC, 128, 2, 128).transpose(2, 1, 0, 3))
    wg4 = np.ascontiguousarray(Wg.reshape(IC, 128, HC, 128).transpose(0, 3, 2, 1))
    wu4 = np.ascontiguousarray(Wu.reshape(IC, 128, HC, 128).transpose(0, 3, 2, 1))
    wdt = np.ascontiguousarray(Wd.T)
    agd = np.ascontiguousarray(ag.reshape(IC, 128))
    aud = np.ascontiguousarray(au.reshape(IC, 128))
    ident = np.eye(128, dtype=np.float32)

    covs = [np.ascontiguousarray(
        np.concatenate([x[A, :COV_A], x[1 - A, :COV_B]], axis=0)) for A in (0, 1)]
    rp_cache = {}
    in_maps, meta = [], []
    for c in range(8):
        A, p = c // 4, c % 4
        slots = []
        for s_ in range(8):
            v = SLOT_V[s_]
            b = A if s_ < 4 else 1 - A
            slots.append((b, 4 * (v - 1) + p))
        x_own = np.ascontiguousarray(
            np.concatenate([x[b, 128 * t:128 * (t + 1)] for (b, t) in slots], axis=0))
        if p not in rp_cache:
            rp_cache[p] = _build_rp(p)
        in_maps.append({
            "x_own": x_own, "x_cov": covs[A], "rp": rp_cache[p],
            "wq4": wq4, "wkv4": wkv4, "wg4": wg4, "wu4": wu4, "wdt": wdt,
            "wo": Wo, "agd": agd, "aud": aud, "add_": ad, "ident": ident,
        })
        meta.append(slots)
    return in_maps, meta


def kernel(**inputs):
    in_maps, meta = _stage(inputs)
    nc = build_nc(dbg=False)
    res = run_bass_kernel_spmd(nc, in_maps, core_ids=list(range(8)))
    y = np.empty((B, S, HID), np.float32)
    for c in range(8):
        oc = res.results[c]["out"]
        for s_, (b, t) in enumerate(meta[c]):
            y[b, 128 * t:128 * (t + 1)] = oc[128 * s_:128 * (s_ + 1)]
    return y
